# version 04
# baseline (speedup 1.0000x reference)
import contextlib
import math

import numpy as np
import ml_dtypes

import concourse.bass as bass
import concourse.mybir as mybir
from concourse.ap import AP
from concourse.bass_utils import run_bass_kernel_spmd

F32 = mybir.dt.float32
BF16 = mybir.dt.bfloat16
ALU = mybir.AluOpType
AF = mybir.ActivationFunctionType
AX = mybir.AxisListType

NCORE = 8
T = 2048
D = 1024
NT = 16
NM = 16
NE = 32
EXP_SHIFT = -30.0


def _bucket_1d():
    delta = np.arange(-2047, 2048, dtype=np.int32)
    n = -delta
    nb = 16
    ret = (n < 0).astype(np.int32) * nb
    n = np.abs(n)
    max_exact = 8
    is_small = n < max_exact
    ratio = np.log(np.maximum(n, 1).astype(np.float32)
                   / np.float32(max_exact)) / np.float32(math.log(128 / 8))
    vl = max_exact + (ratio * np.float32(nb - max_exact)).astype(np.int32)
    vl = np.minimum(vl, nb - 1)
    return ret + np.where(is_small, n, vl)


def _host_prep(inputs):
    ct = np.ascontiguousarray
    p = {}
    p["x_all"] = ct(np.asarray(inputs["L0_attn_x"], dtype=np.float32))
    for nm, key in [("wq", "L0_attn_Wq"), ("wk", "L0_attn_Wk"),
                    ("wv", "L0_attn_Wv"), ("wo", "L0_attn_Wo"),
                    ("w1", "L0_attn_W1"), ("w2", "L0_attn_W2"),
                    ("sg", "L0_ffn_shared_gate_w"), ("su", "L0_ffn_shared_up_w"),
                    ("sd", "L0_ffn_shared_down_w")]:
        p[nm] = ct(np.asarray(inputs[key], dtype=np.float32).T)
    p["gw"] = ct(np.asarray(inputs["L0_ffn_gate_weight"], np.float32).T)
    p["gu"] = ct(np.transpose(
        np.asarray(inputs["L0_ffn_experts_gate_up"], np.float32), (0, 2, 1)))
    p["dn"] = ct(np.transpose(
        np.asarray(inputs["L0_ffn_experts_down"], np.float32), (0, 2, 1)))
    for nm, key in [("bq", "L0_attn_bq"), ("bk", "L0_attn_bk"),
                    ("bv", "L0_attn_bv"), ("bo", "L0_attn_bo"),
                    ("b1", "L0_attn_b1"), ("b2", "L0_attn_b2")]:
        p[nm] = ct(np.asarray(inputs[key], np.float32).reshape(1, D))
    p["ebias"] = ct(np.asarray(
        inputs["L0_ffn_e_score_correction_bias"], np.float32).reshape(1, NE))

    buckets = _bucket_1d()
    rb = np.asarray(inputs["L0_attn_rel_bias"], np.float32)
    band = rb[buckets, :]
    band = np.concatenate([band, np.zeros((1, 16), band.dtype)], 0)
    idx = (16 * np.arange(256)[None, :] + np.arange(31)[:, None])
    idx = np.minimum(idx, 4095)
    hv = band[idx, :]
    p["hv"] = ct(np.transpose(hv, (2, 0, 1)).astype(np.float32))

    bf = ml_dtypes.bfloat16
    p["identf"] = np.eye(128, dtype=np.float32)
    p["identb"] = np.eye(128, dtype=bf)
    p["revb"] = np.eye(128, dtype=np.float32)[::-1].astype(bf)
    p["onesb"] = np.ones((1, 128), dtype=bf)
    return p


class _Sem:
    __slots__ = ("h", "n")


class _B:
    def __init__(self, nc, st):
        self.nc = nc
        self.st = st
        self.q = {"sync": [], "vector": [], "scalar": [], "tensor": [],
                  "gpsimd": []}

    def sem(self, name):
        s = _Sem()
        s.h = self.st.enter_context(self.nc.semaphore(name))
        s.n = 0
        return s

    def E(self, eng, fn, wait=(), inc=()):
        for s, k in inc:
            s.n += k
        self.q[eng].append((fn, tuple((s.h, v) for s, v in wait),
                            tuple((s.h, k) for s, k in inc)))

    def W(self, eng, *waits):
        self.q[eng].append((None, tuple((s.h, v) for s, v in waits), ()))


_PROG = None
_DBG = {}
DEBUG = False


def _build():
    nc = bass.Bass("TRN2", target_bir_lowering=False,
                   dynamic_dma_scratch_size=16384)
    din = {}
    for name, shape, dt in [
        ("x", [T, D], F32), ("wq", [D, D], F32), ("wk", [D, D], F32),
        ("wv", [D, D], F32), ("wo", [D, D], F32), ("w1", [D, D], F32),
        ("w2", [D, D], F32), ("sg", [D, D], F32), ("su", [D, D], F32),
        ("sd", [D, D], F32), ("gw", [D, NE], F32),
        ("gu", [NE, D, D], F32), ("dn", [NE, 512, D], F32),
        ("bq", [1, D], F32), ("bk", [1, D], F32), ("bv", [1, D], F32),
        ("bo", [1, D], F32), ("b1", [1, D], F32), ("b2", [1, D], F32),
        ("ebias", [1, NE], F32), ("hv", [16, 31, 256], F32),
        ("identf", [128, 128], F32), ("identb", [128, 128], BF16),
        ("revb", [128, 128], BF16), ("onesb", [1, 128], BF16),
    ]:
        din[name] = nc.dram_tensor(name, shape, dt, kind="ExternalInput")
    out_d = nc.dram_tensor("out", [T, D], F32, kind="ExternalOutput")
    hvb_d = nc.dram_tensor("hvb", [16 * 31 * 256], BF16, kind="Internal")
    s_d = nc.dram_tensor("sdram", [T, D], F32,
                         kind="ExternalOutput" if DEBUG else "Internal")
    if DEBUG:
        dbg_xn = nc.dram_tensor("dbg_xn", [128, 16384], F32, kind="ExternalOutput")
        dbg_sc = nc.dram_tensor("dbg_sc", [128, 1024], F32, kind="ExternalOutput")
        dbg_acc = nc.dram_tensor("dbg_acc", [128, 16384], F32, kind="ExternalOutput")
        dbg_att = nc.dram_tensor("dbg_att", [128, 16384], F32, kind="ExternalOutput")

    st = contextlib.ExitStack()
    with st:
        S = lambda *a: st.enter_context(nc.sbuf_tensor(*a))
        c8 = S("c8", [128, 8], F32)
        identf = S("identf_s", [128, 128], F32)
        revb = S("revb_s", [128, 128], BF16)
        onesb = S("onesb_s", [1, 128], BF16)
        onesf = S("onesf_s", [1, 128], F32)
        bias_pp = S("bias_pp", [128, 24], F32)
        ebias_bc = S("ebias_bc", [128, NE], F32)
        stats = S("stats", [128, 32], F32)
        r_sb = S("r_sb", [128, 160], F32)
        combine = S("combine", [128, 512], F32)
        xn_fm = S("xn_fm", [128, 16384], BF16)
        x_sb = S("x_sb", [128, 1024], F32)
        wstage = S("wstage", [128, 1024], F32)
        xn32 = S("xn32", [128, 1024], F32)
        zbuf = S("zbuf", [128, 1024], F32)
        UH = S("UH", [128, 16384], BF16)
        abt = S("abt", [128, 1024], F32)
        outt = S("outt", [128, 1024], F32)
        UW = S("UW", [128, 12288], F32)
        ACCU = S("ACCU", [128, 16384], F32)

        bf16 = lambda ap: ap.bitcast(BF16)
        logits = ACCU[:, 0:2048][:, 0:512]
        E2 = bf16(ACCU[:, 2048:4096])
        qm = bf16(ACCU[:, 4096:4608])
        kmA = bf16(ACCU[:, 4608:5120])
        kmB = bf16(ACCU[:, 5120:5632])
        vm = bf16(ACCU[:, 5632:6144])
        diag2 = bf16(ACCU[:, 6144:6272])
        hst2 = bf16(ACCU[:, 6272:6400])
        partials = ACCU[:, 6400:6408]
        PT = bf16(ACCU[:, 6656:10752])
        UBT = ACCU[:, 10752:14848]
        scores = ACCU[:, 512:1024]
        ffb = bf16(ACCU[:, 1024:2048])
        gw_sb = ACCU[:, 2048:2304]
        brow_st = ACCU[0:8, 6408:6536]
        mvp = ACCU[:, 6536:6552]
        wA = bf16(UW[:, 0:4096])
        wB = bf16(UW[:, 4096:8192])
        wC = bf16(UW[:, 8192:12288])
        guW = [bf16(UW[:, 0:4096]), bf16(UW[:, 4096:8192])]
        dnW = [bf16(UW[:, 8192:10240]), bf16(UW[:, 10240:12288])]
        sdW = bf16(UW[:, 8192:12288])
        brow = ACCU[0:1, 14848:16384].bitcast(BF16)
        ffA = ffb[:, 0:1024]
        ffB = ffb[:, 1024:2048]
        acc = ACCU

        PS = st.enter_context(nc.psum_tensor("PSA", [128, 4096], F32))
        _DBG.update(dict(UH=UH, ACCU=ACCU, UW=UW, xn_fm=xn_fm, UBT=UBT,
                         scores=scores, combine=combine, stats=stats))

        b = _B(nc, st)
        s_pro = b.sem("s_pro")
        s_w0 = b.sem("s_w0")
        s_w1 = b.sem("s_w1")
        s_hk0 = b.sem("s_hk0")
        s_hk1 = b.sem("s_hk1")
        s_kmb = b.sem("s_kmb")
        s_row = b.sem("s_row")
        s_bst = b.sem("s_bst")
        s_x = b.sem("s_x")
        s_do = b.sem("s_do")
        s_pe = b.sem("s_pe")
        s_act = b.sem("s_act")
        s_dve = b.sem("s_dve")
        s_pool = b.sem("s_pool")
        E = b.E
        V, A, P, X, G = "vector", "scalar", "tensor", "sync", "gpsimd"

        cvals = [EXP_SHIFT, 1e-5, -0.5, 1e-20, 2.5, -1e30, 1e30, 0.0]
        for i, v in enumerate(cvals):
            E(V, lambda i=i, v=v: nc.vector.memset(c8[:, i:i + 1], v))
        E(V, lambda: nc.vector.memset(onesf[:, :], 1.0), inc=[(s_dve, 1)])
        consts_ok = (s_dve, s_dve.n)
        c_shift = c8[:, 0:1]
        c_eps = c8[:, 1:2]
        c_mhalf = c8[:, 2:3]
        c_tiny = c8[:, 3:4]
        c_25 = c8[:, 4:5]
        c_msent = c8[:, 5:6]

        def _dma(eng, sem, dst, src, wait=()):
            E(eng, lambda d=dst, s=src: getattr(nc, eng).dma_start(out=d, in_=s),
              wait=wait, inc=[(sem, 16)])

        for dst, src in [(identf[:, :], din["identf"].ap()),
                         (revb[:, :], din["revb"].ap()),
                         (onesb[:, :], din["onesb"].ap()),
                         (ebias_bc[0:1, :], din["ebias"].ap())]:
            _dma(X, s_pro, dst, src)
        gw_src = AP(din["gw"], 0, [[NE, 128], [128 * NE, 8], [1, NE]])
        gw_dst = gw_sb[:, 0:256].rearrange("p (c e) -> p c e", e=NE)
        _dma(X, s_pro, gw_dst, gw_src)
        hv_flat = din["hv"].ap().rearrange("a b c -> (a b c)")
        hv_f = UH.bitcast(F32)[:, 0:992]
        _dma(X, s_pro, hv_f, hv_flat.rearrange("(p k) -> p k", p=128))
        n_pro_dw = s_pro.n
        hv_b = UH[:, 2048:3040]
        E(G, lambda: nc.gpsimd.tensor_copy(hv_b, hv_f),
          wait=[(s_pro, n_pro_dw)], inc=[(s_pool, 1)])
        E(X, lambda: nc.sync.dma_start(
            out=hvb_d.ap().rearrange("(p k) -> p k", p=128), in_=hv_b),
          wait=[(s_pool, s_pool.n)], inc=[(s_pro, 16)])
        hv_done = (s_pro, s_pro.n)
        xsb_free = hv_done

        for j, nm in enumerate(["bv", "bo", "b2"]):
            _dma(X, s_row, UBT[0:1, j * 1024:(j + 1) * 1024], din[nm].ap())
        rows_dw = (s_row, s_row.n)
        for j in range(3):
            E(V, lambda j=j: nc.vector.tensor_copy(
                brow[0:1, j * 1024:(j + 1) * 1024],
                UBT[0:1, j * 1024:(j + 1) * 1024]),
              wait=[rows_dw] if j == 0 else (), inc=[(s_dve, 1)])
        brow_done = (s_dve, s_dve.n)

        ident_ok = (s_pro, n_pro_dw)
        for j, nm in enumerate(["bq", "bk", "b1"]):
            w_d = [] if j == 0 else [(s_pe, s_pe.n)]
            _dma(X, s_bst, brow_st[:, :],
                 din[nm].ap().rearrange("a (c p) -> (a c) p", p=128), wait=w_d)
            w_ld = (s_bst, s_bst.n)
            E(P, lambda j=j: nc.tensor.transpose(
                PS[:, 3584 + j * 128:3584 + j * 128 + 8], brow_st[:, :],
                identf[0:8, 0:8]),
              wait=[w_ld, ident_ok, (s_dve, s_dve.n)], inc=[(s_pe, 1)])
            pe_t = (s_pe, s_pe.n)
            E(V, lambda j=j: nc.vector.tensor_copy(
                bias_pp[:, j * 8:(j + 1) * 8],
                PS[:, 3584 + j * 128:3584 + j * 128 + 8]),
              wait=[pe_t], inc=[(s_dve, 1)])
        E(P, lambda: nc.tensor.matmul(PS[:, 3072:3104], onesf[0:1, :],
                                      ebias_bc[0:1, :], start=True, stop=True),
          wait=[(s_pro, n_pro_dw), consts_ok], inc=[(s_pe, 1)])
        E(V, lambda: nc.vector.tensor_copy(ebias_bc[:, :], PS[:, 3072:3104]),
          wait=[(s_pe, s_pe.n)], inc=[(s_dve, 1)])
        pro_psum_evac = (s_dve, s_dve.n)

        pool_hist = []

        def _wload(key, tgt, first_wait=None, cast_wait=None, src3=None):
            for cc in range(16):
                dc, half = cc // 2, cc % 2
                slot = cc % 2
                w_d = []
                if first_wait is not None and cc == 0:
                    w_d.append(first_wait)
                if len(pool_hist) >= 2:
                    w_d.append((s_pool, pool_hist[-2]))
                if src3 is None:
                    src = din[key].ap()[dc * 128:(dc + 1) * 128,
                                        half * 512:(half + 1) * 512]
                else:
                    src = src3[dc * 128:(dc + 1) * 128,
                               half * 512:(half + 1) * 512]
                sw = s_w0 if slot == 0 else s_w1
                E(X, lambda d=wstage[:, slot * 512:(slot + 1) * 512], s=src:
                  nc.sync.dma_start(out=d, in_=s), wait=w_d, inc=[(sw, 16)])
                dma_ok = (sw, sw.n)
                cw = [dma_ok]
                if cast_wait is not None and cc == 0:
                    cw.append(cast_wait)
                E(G, lambda tgt=tgt, dc=dc, half=half, slot=slot:
                  nc.gpsimd.tensor_copy(
                      tgt[:, dc * 1024 + half * 512:dc * 1024 + (half + 1) * 512],
                      wstage[:, slot * 512:(slot + 1) * 512]),
                  wait=cw, inc=[(s_pool, 1)])
                pool_hist.append(s_pool.n)

        _wload("wq", wA)
        _wload("wk", wB)
        _wload("wv", wC)
        wqkv_pool = (s_pool, s_pool.n)

        xt_consumed = {}
        prev_evac = pro_psum_evac
        prev_tr7 = None
        prev_ln = None
        for t in range(NT):
            waits = [xsb_free]
            if t >= 1:
                waits.append(xt_consumed[t - 1])
            E(G, lambda t=t: nc.gpsimd.dma_start(
                out=x_sb[:, :],
                in_=din["x"].ap()[t * 128:(t + 1) * 128, :]),
              wait=waits, inc=[(s_x, 16)])
            x_ready = (s_x, s_x.n)
            xs = x_sb[:, 0:1024]
            E(V, lambda xs=xs: nc.vector.bn_stats(mvp[:, 0:6], xs[:, 0:512]),
              wait=[x_ready, (s_dve, s_dve.n)], inc=[(s_dve, 1)])
            E(V, lambda xs=xs: nc.vector.bn_stats(mvp[:, 6:12], xs[:, 512:1024]),
              wait=[(s_dve, s_dve.n)], inc=[(s_dve, 1)])
            E(V, lambda: nc.vector.bn_aggr(mvp[:, 12:14], mvp[:, 0:12]),
              wait=[(s_dve, s_dve.n)], inc=[(s_dve, 1)])
            ts_w = [(s_dve, s_dve.n), consts_ok]
            if prev_ln is not None:
                ts_w.append(prev_ln)
            E(V, lambda: nc.vector.tensor_scalar(
                mvp[:, 14:15], mvp[:, 13:14], c_eps, None, ALU.add),
              wait=ts_w, inc=[(s_dve, 1)])
            v_mv = (s_dve, s_dve.n)
            E(A, lambda: nc.scalar.activation(mvp[:, 15:16], mvp[:, 14:15],
                                              AF.Ln),
              wait=[v_mv], inc=[(s_act, 1)])
            prev_ln = (s_act, s_act.n)
            E(A, lambda t=t: nc.scalar.activation(
                stats[:, 2 * t + 1:2 * t + 2], mvp[:, 15:16], AF.Exp,
                scale=c_mhalf), wait=[(s_act, s_act.n)], inc=[(s_act, 1)])
            rstd_done = (s_act, s_act.n)
            E(V, lambda t=t: nc.vector.tensor_copy(
                stats[:, 2 * t:2 * t + 1], mvp[:, 12:13]),
              wait=[(s_dve, s_dve.n)], inc=[(s_dve, 1)])
            tsp_w = [rstd_done, (s_dve, s_dve.n)]
            if prev_tr7 is not None:
                tsp_w.append(prev_tr7)
            E(V, lambda xs=xs, t=t: nc.vector.tensor_scalar(
                xn32[:, :], xs, stats[:, 2 * t:2 * t + 1],
                stats[:, 2 * t + 1:2 * t + 2], ALU.subtract, ALU.mult),
              wait=tsp_w, inc=[(s_dve, 1)])
            xt_consumed[t] = (s_dve, s_dve.n)
            xn_ready = (s_dve, s_dve.n)
            for c in range(8):
                off = (c // 4) * 512 + (c % 4) * 128
                w_d = [xn_ready] if c == 0 else []
                if c == 0:
                    w_d.append(prev_evac)
                E(P, lambda c=c, off=off: nc.tensor.transpose(
                    PS[:, off:off + 128], xn32[:, c * 128:(c + 1) * 128],
                    identf[:, :]), wait=w_d,
                  inc=[(s_pe, 1)] if c in (3, 7) else ())
                if c == 3 or c == 7:
                    tr_done = (s_pe, s_pe.n)
                    base = (c // 4) * 512
                    c0 = c - 3
                    E(V, lambda t=t, base=base, c0=c0: nc.vector.tensor_copy(
                        xn_fm[:, :].rearrange("p (k w) -> p k w", w=2048)
                        [:, c0:c0 + 4, t * 128:(t + 1) * 128],
                        PS[:, base:base + 512].rearrange(
                            "p (k w) -> p k w", w=128)),
                      wait=[tr_done], inc=[(s_dve, 1)])
                    E(V, lambda base=base, c0=c0: nc.vector.tensor_copy(
                        zbuf[:, c0 * 128:(c0 + 4) * 128], PS[:, base:base + 512]),
                      wait=[(s_dve, s_dve.n)], inc=[(s_dve, 1)])
            prev_tr7 = (s_pe, s_pe.n)
            ev_done = (s_dve, s_dve.n)
            for c in range(8):
                w_d = [ev_done, (s_pro, n_pro_dw)] if c == 0 else []
                E(P, lambda c=c: nc.tensor.matmul(
                    PS[:, 3072:3104], zbuf[:, c * 128:(c + 1) * 128],
                    gw_sb[:, c * 32:(c + 1) * 32],
                    start=(c == 0), stop=(c == 7)),
                  wait=w_d, inc=[(s_pe, 1)] if c == 7 else ())
            gate_done = (s_pe, s_pe.n)
            E(V, lambda t=t: nc.vector.tensor_copy(
                logits[:, t * 32:(t + 1) * 32], PS[:, 3072:3104]),
              wait=[gate_done], inc=[(s_dve, 1)])
            prev_evac = (s_dve, s_dve.n)
        phase0_dve = (s_dve, s_dve.n)

        E(A, lambda: nc.scalar.activation(scores[:, :], logits[:, :],
                                          AF.Sigmoid),
          wait=[phase0_dve], inc=[(s_act, 1)])
        sig_done = (s_act, s_act.n)

        sfc = r_sb[:, 0:32]
        gsp = r_sb[:, 32:44]
        mk = r_sb[:, 48:82]
        tw = r_sb[:, 96:128]
        gmask = r_sb[:, 128:136]
        t8 = r_sb[:, 136:144]
        tsA = r_sb[:, 144:145]
        tsB = r_sb[:, 145:146]
        tsC = r_sb[:, 146:147]

        def _rt(fn, extra=()):
            E(V, fn, wait=tuple(extra) + ((s_dve, s_dve.n),), inc=[(s_dve, 1)])

        gm_b = AP(r_sb[:].tensor, r_sb[:].offset + 128,
                  [[r_sb[:].ap[0][0], 128], [1, 8], [0, 4]])
        for t in range(NT):
            sc_t = scores[:, t * 32:(t + 1) * 32]
            _rt(lambda sc_t=sc_t: nc.vector.tensor_add(sfc, sc_t,
                                                       ebias_bc[:, :]),
                extra=[sig_done] if t == 0 else ())
            _rt(lambda: nc.vector.reduce_max(
                gsp[:, 0:8], sfc.rearrange("p (a b) -> p a b", b=4), axis=AX.X))
            _rt(lambda: nc.vector.memset(gsp[:, 8:12], 1e30))
            _rt(lambda: nc.vector.max(t8[:, :], gsp[:, 0:12]))
            _rt(lambda: nc.vector.match_replace(gsp[:, 0:12], t8[:, :],
                                                gsp[:, 0:12], -1e30))
            _rt(lambda: nc.vector.tensor_scalar(
                gmask, gsp[:, 0:8], c_msent, None, ALU.is_equal))
            _rt(lambda: nc.vector.tensor_tensor(
                mk[:, 0:32].rearrange("p (a b) -> p a b", b=4),
                sfc.rearrange("p (a b) -> p a b", b=4), gm_b, ALU.mult))
            _rt(lambda: nc.vector.memset(mk[:, 32:34], 1e30))
            _rt(lambda: nc.vector.max(t8[:, :], mk[:, :]))
            _rt(lambda: nc.vector.match_replace(mk[:, :], t8[:, :],
                                                mk[:, :], -1e30))
            _rt(lambda: nc.vector.tensor_scalar(
                sfc, mk[:, 0:32], c_msent, None, ALU.is_equal))
            _rt(lambda sc_t=sc_t: nc.vector.tensor_mul(tw, sfc, sc_t))
            _rt(lambda: nc.vector.reduce_sum(tsA, tw, axis=AX.X))
            _rt(lambda: nc.vector.tensor_scalar(tsB, tsA, c_tiny, None,
                                                ALU.add))
            _rt(lambda: nc.vector.reciprocal(tsC, tsB))
            _rt(lambda: nc.vector.tensor_scalar(tsB, tsC, c_25, None,
                                                ALU.mult))
            _rt(lambda t=t: nc.vector.tensor_scalar(
                combine[:, t * 32:(t + 1) * 32], tw, tsB, None, ALU.mult))
        routing_done = (s_dve, s_dve.n)

        att_fm = UH
        row_groups = [[0, 2, 4, 6], [1, 3, 5, 7], [8, 10, 12, 14],
                      [9, 11, 13, 15]]
        m_PV_end = {}
        prev_exp1_cur = None
        prev_exp2 = None
        prev_tr_end = None
        trp_evac = {}
        pv_evac = {}
        last_qkv_all = None

        for m in range(NM):
            bt_pe = {}
            bt_act = {}
            for di in range(31):
                slot = di % 2
                hsrc = AP(hvb_d, (m * 31 + di) * 256, [[1, 128], [1, 128]])
                w_d = []
                if di == 0:
                    w_d = [hv_done, brow_done]
                    if m > 0:
                        w_d.append(m_PV_end[m - 1])
                if di >= 2:
                    w_d.append((s_pe, bt_pe[di - 2]))
                shk = s_hk0 if slot == 0 else s_hk1
                E(G, lambda d=hst2[:, slot * 128:(slot + 1) * 128], s=hsrc:
                  nc.gpsimd.dma_start(out=d, in_=s), wait=w_d, inc=[(shk, 16)])
                h_ok = (shk, shk.n)
                pw = [h_ok]
                if di == 0:
                    pw.append(pro_psum_evac if m == 0 else m_PV_end[m - 1])
                    if m == 0:
                        pw.append(phase0_dve)
                if di >= 2:
                    pw.append((s_act, bt_act[di - 2]))
                boff = 3072 + (di % 2) * 512 + ((di // 2) % 2) * 128
                E(P, lambda slot=slot, boff=boff: nc.tensor.matmul(
                    PS[:, boff:boff + 128],
                    revb[:, :], hst2[:, slot * 128:(slot + 1) * 128],
                    start=True, stop=True), wait=pw, inc=[(s_pe, 1)])
                bt_pe[di] = s_pe.n
                E(A, lambda di=di, boff=boff: nc.scalar.copy(
                    UBT[:, (30 - di) * 128:(31 - di) * 128],
                    PS[:, boff:boff + 128]),
                  wait=[(s_pe, bt_pe[di])], inc=[(s_act, 1)])
                bt_act[di] = s_act.n
            bt_ready = (s_act, s_act.n)

            qkv_evac = {}
            qkv_seq = []
            for which, (wsb, dst, bcol) in enumerate([(wA, qm, 0),
                                                      (wB, kmA, 8)]):
                for c in range(8):
                    off = 3072 + (c % 2) * 512 + ((c // 2) % 2) * 128
                    for kc in range(8):
                        w_d = []
                        if kc == 0:
                            if which == 0 and c == 0:
                                w_d = [phase0_dve, wqkv_pool, routing_done,
                                       bt_ready]
                                if m > 0:
                                    w_d.append(m_PV_end[m - 1])
                            elif len(qkv_seq) >= 2:
                                w_d.append(qkv_seq[-2])
                        E(P, lambda wsb=wsb, c=c, kc=kc, m=m, off=off:
                          nc.tensor.matmul(
                              PS[:, off:off + 128],
                              wsb[:, kc * 1024 + c * 128:
                                  kc * 1024 + (c + 1) * 128],
                              xn_fm[:, kc * 2048 + m * 128:
                                    kc * 2048 + (m + 1) * 128],
                              start=(kc == 0), stop=(kc == 7)),
                          wait=w_d, inc=[(s_pe, 1)] if kc == 7 else ())
                    acc_done = (s_pe, s_pe.n)
                    E(V, lambda dst=dst, c=c, off=off, bcol=bcol:
                      nc.vector.tensor_scalar(
                          dst[:, c * 128:(c + 1) * 128], PS[:, off:off + 128],
                          bias_pp[:, bcol + c:bcol + c + 1], None, ALU.add),
                      wait=[acc_done], inc=[(s_dve, 1)])
                    qkv_evac[(which, c)] = (s_dve, s_dve.n)
                    qkv_seq.append((s_dve, s_dve.n))
            for os in range(2):
                off = 3072 + os * 512
                for kc in range(8):
                    w_d = []
                    if kc == 0:
                        w_d = [qkv_evac[(1, 7)], brow_done]
                    E(P, lambda os=os, kc=kc, m=m, off=off: nc.tensor.matmul(
                        PS[:, off:off + 512],
                        xn_fm[:, kc * 2048 + m * 128:kc * 2048 + (m + 1) * 128],
                        wC[:, kc * 1024 + os * 512:kc * 1024 + (os + 1) * 512],
                        start=(kc == 0), stop=False), wait=w_d)
                E(P, lambda os=os, off=off: nc.tensor.matmul(
                    PS[:, off:off + 512], onesb[0:1, :],
                    brow[0:1, os * 512:(os + 1) * 512],
                    start=False, stop=True), inc=[(s_pe, 1)])
                v_done = (s_pe, s_pe.n)
                E(V, lambda os=os, off=off: nc.vector.tensor_copy(
                    vm[:, os * 512:(os + 1) * 512], PS[:, off:off + 512]),
                  wait=[v_done], inc=[(s_dve, 1)])
                qkv_evac[(2, os)] = (s_dve, s_dve.n)
            if m == NM - 1:
                last_qkv_all = (s_dve, s_dve.n)
            kmA_ok = qkv_evac[(1, 7)]
            E(G, lambda: nc.gpsimd.dma_start(out=kmB[0:64, :],
                                             in_=kmA[64:128, :]),
              wait=[kmA_ok], inc=[(s_kmb, 16)])
            E(G, lambda: nc.gpsimd.dma_start(out=kmB[64:128, :],
                                             in_=kmA[0:64, :]),
              inc=[(s_kmb, 16)])
            kmB_ok = (s_kmb, s_kmb.n)
            q_ok = qkv_evac[(2, 1)]

            for gi, grp in enumerate(row_groups):
                for ri, fb_a in enumerate(grp):
                    par = fb_a % 2
                    rowid = gi * 4 + ri
                    eslot = rowid % 2
                    Erow = E2[:, eslot * 2048:(eslot + 1) * 2048]
                    for fb_b in range(16):
                        w_d = []
                        if fb_b == 0:
                            w_d = [q_ok, kmB_ok]
                            if prev_exp2 is not None:
                                w_d.append(prev_exp2)
                        if fb_b == 8 and prev_exp1_cur is not None:
                            w_d.append(prev_exp1_cur)
                        km_use = kmA if (fb_b % 2) == par else kmB
                        E(P, lambda fb_a=fb_a, fb_b=fb_b, par=par,
                          km_use=km_use: nc.tensor.matmul(
                              PS[:, fb_b * 128:(fb_b + 1) * 128],
                              qm[par * 64:par * 64 + 64,
                                 (fb_a // 2) * 128:(fb_a // 2 + 1) * 128],
                              km_use[par * 64:par * 64 + 64,
                                     (fb_b // 2) * 128:(fb_b // 2 + 1) * 128],
                              start=True, stop=True,
                              tile_position=(par * 64, 0)),
                          wait=w_d,
                          inc=[(s_pe, 1)] if fb_b in (7, 15) else ())
                        if fb_b == 7:
                            s_half1 = (s_pe, s_pe.n)
                        elif fb_b == 15:
                            s_half2 = (s_pe, s_pe.n)
                    E(A, lambda Erow=Erow: nc.scalar.activation(
                        Erow[:, 0:1024], PS[:, 0:1024], AF.Exp, bias=c_shift,
                        accum_out=partials[:, 0:1]),
                      wait=[s_half1, (s_dve, s_dve.n)], inc=[(s_act, 1)])
                    prev_exp1_cur = (s_act, s_act.n)
                    E(A, lambda Erow=Erow: nc.scalar.activation(
                        Erow[:, 1024:2048], PS[:, 1024:2048], AF.Exp,
                        bias=c_shift, accum_out=partials[:, 1:2]),
                      wait=[s_half2], inc=[(s_act, 1)])
                    prev_exp2 = (s_act, s_act.n)
                    dslot = eslot
                    E(V, lambda: nc.vector.tensor_add(
                        partials[:, 2:3], partials[:, 0:1], partials[:, 1:2]),
                      wait=[prev_exp2, (s_dve, s_dve.n)], inc=[(s_dve, 1)])
                    E(V, lambda: nc.vector.reciprocal(
                        partials[:, 3:4], partials[:, 2:3]),
                      wait=[(s_dve, s_dve.n)], inc=[(s_dve, 1)])
                    E(V, lambda dslot=dslot: nc.vector.tensor_scalar(
                        diag2[:, dslot * 128:(dslot + 1) * 128], identf[:, :],
                        partials[:, 3:4], None, ALU.mult),
                      wait=[(s_dve, s_dve.n)], inc=[(s_dve, 1)])
                    diag_ok = (s_dve, s_dve.n)
                    for qd in range(4):
                        pbase = 2048 + (qd % 2) * 512
                        for j in range(4):
                            fb_b = qd * 4 + j
                            w_d = []
                            if fb_b == 0:
                                w_d.append(diag_ok)
                            if j == 0:
                                key = (rowid, qd - 2) if qd >= 2 else \
                                      (rowid - 1, qd + 2)
                                if key in trp_evac:
                                    w_d.append(trp_evac[key])
                            E(P, lambda Erow=Erow, fb_b=fb_b, pbase=pbase,
                              j=j, dslot=dslot: nc.tensor.matmul(
                                  PS[:, pbase + j * 128:pbase + (j + 1) * 128],
                                  Erow[:, fb_b * 128:(fb_b + 1) * 128],
                                  diag2[:, dslot * 128:(dslot + 1) * 128],
                                  start=True, stop=True),
                              wait=w_d, inc=[(s_pe, 1)] if j == 3 else ())
                        tq_done = (s_pe, s_pe.n)
                        E(V, lambda fb_a=fb_a, qd=qd, pbase=pbase, ri=ri:
                          nc.vector.tensor_tensor(
                              PT[:, :].rearrange("p (bb w) -> p bb w", w=512)
                              [:, qd * 4:(qd + 1) * 4,
                               ri * 128:(ri + 1) * 128],
                              PS[:, pbase:pbase + 512].rearrange(
                                  "p (bb w) -> p bb w", w=128),
                              UBT[:, (15 - fb_a + qd * 4) * 128:
                                  (15 - fb_a + qd * 4 + 4) * 128].rearrange(
                                  "p (bb w) -> p bb w", w=128),
                              ALU.add), wait=[tq_done, bt_ready],
                          inc=[(s_dve, 1)])
                        trp_evac[(rowid, qd)] = (s_dve, s_dve.n)
                    prev_tr_end = (s_pe, s_pe.n)
                pvpar = grp[0] % 2
                pbase_pv = 3072 if gi in (0, 2) else 3584
                outp = PS[pvpar * 64:pvpar * 64 + 64, pbase_pv:pbase_pv + 512]
                for fb_b in range(16):
                    w_d = []
                    if fb_b == 0:
                        w_d = [trp_evac[(gi * 4 + 3, 3)]]
                        if (m, gi) != (0, 0):
                            pass
                        if gi >= 2:
                            w_d.append(pv_evac[gi - 2])
                    E(P, lambda fb_b=fb_b, outp=outp, pvpar=pvpar:
                      nc.tensor.matmul(
                          outp, vm[:, fb_b * 64:(fb_b + 1) * 64],
                          PT[:, fb_b * 512:(fb_b + 1) * 512],
                          start=(fb_b == 0), stop=(fb_b == 15),
                          tile_position=(0, pvpar * 64)),
                      wait=w_d, inc=[(s_pe, 1)] if fb_b == 15 else ())
                pv_done = (s_pe, s_pe.n)
                c0 = 0 if grp[0] < 8 else 4
                E(V, lambda m=m, c0=c0, pvpar=pvpar, outp=outp:
                  nc.vector.tensor_copy(
                      att_fm[pvpar * 64:pvpar * 64 + 64, :]
                      .rearrange("p (cc w) -> p cc w", w=2048)
                      [:, c0:c0 + 4, m * 128:(m + 1) * 128],
                      outp.rearrange("p (cc w) -> p cc w", w=128)),
                  wait=[pv_done], inc=[(s_dve, 1)])
                pv_evac[gi] = (s_dve, s_dve.n)
            m_PV_end[m] = pv_evac[3]
        attA_done_dve = (s_dve, s_dve.n)
        if DEBUG:
            E(G, lambda: nc.gpsimd.dma_start(
                out=dbg_att.ap().bitcast(BF16)[:, 0:16384][:, 0:16384],
                in_=UH[:, 0:16384]),
              wait=[attA_done_dve], inc=[(s_kmb, 16)])
            E(G, lambda: nc.gpsimd.dma_start(
                out=dbg_xn.ap().bitcast(BF16)[:, 0:16384],
                in_=xn_fm[:, 0:16384]),
              wait=[(s_kmb, s_kmb.n)], inc=[(s_kmb, 16)])
            E(G, lambda: nc.gpsimd.dma_start(
                out=dbg_sc.ap()[:, 0:512], in_=scores[:, :]),
              wait=[routing_done], inc=[(s_kmb, 16)])
            E(G, lambda: nc.gpsimd.dma_start(
                out=dbg_sc.ap()[:, 512:1024], in_=combine[:, :]),
              inc=[(s_kmb, 16)])

        pool_hist.clear()
        _wload("wo", wA, first_wait=last_qkv_all)
        _wload("w1", wB)
        _wload("w2", wC)
        wB_pool = (s_pool, s_pool.n)

        mean0 = r_sb[:, 148:149]
        rstd0 = r_sb[:, 149:150]
        _ln_prev = [prev_ln]

        def _ln0(src):
            E(V, lambda src=src: nc.vector.bn_stats(mvp[:, 0:6], src[:, 0:512]),
              wait=[(s_dve, s_dve.n)], inc=[(s_dve, 1)])
            E(V, lambda src=src: nc.vector.bn_stats(mvp[:, 6:12],
                                                    src[:, 512:1024]),
              wait=[(s_dve, s_dve.n)], inc=[(s_dve, 1)])
            agg_w = [(s_dve, s_dve.n)]
            if _ln_prev[0] is not None:
                agg_w.append(_ln_prev[0])
            E(V, lambda: nc.vector.bn_aggr(mvp[:, 12:14], mvp[:, 0:12]),
              wait=agg_w, inc=[(s_dve, 1)])
            E(V, lambda: nc.vector.tensor_copy(mean0, mvp[:, 12:13]),
              wait=[(s_dve, s_dve.n)], inc=[(s_dve, 1)])
            v_ok = (s_dve, s_dve.n)
            E(A, lambda: nc.scalar.activation(mvp[:, 15:16], mvp[:, 13:14],
                                              AF.Ln),
              wait=[v_ok], inc=[(s_act, 1)])
            _ln_prev[0] = (s_act, s_act.n)
            E(A, lambda: nc.scalar.activation(rstd0, mvp[:, 15:16], AF.Exp,
                                              scale=c_mhalf),
              wait=[(s_act, s_act.n)], inc=[(s_act, 1)])
            return (s_act, s_act.n)

        bx_done = {}
        bz_evac = None
        ff1_last = None
        s_spill = None
        for t in range(NT):
            w_d = [attA_done_dve] if t < 1 else [(s_do, s_do.n)]
            E(G, lambda t=t: nc.gpsimd.dma_start(
                out=x_sb[:, :],
                in_=din["x"].ap()[t * 128:(t + 1) * 128, :]),
              wait=w_d, inc=[(s_x, 16)])
            xB_ready = (s_x, s_x.n)
            xs = x_sb[:, 0:1024]
            for os in range(2):
                for kc in range(8):
                    w_d = []
                    if os == 0 and kc == 0:
                        w_d = [attA_done_dve, wB_pool, brow_done]
                        if bz_evac is not None:
                            w_d.append(bz_evac)
                    E(P, lambda t=t, os=os, kc=kc: nc.tensor.matmul(
                        PS[:, os * 512:(os + 1) * 512],
                        att_fm[:, kc * 2048 + t * 128:
                               kc * 2048 + (t + 1) * 128],
                        wA[:, kc * 1024 + os * 512:kc * 1024 + (os + 1) * 512],
                        start=(kc == 0), stop=False), wait=w_d)
                E(P, lambda os=os: nc.tensor.matmul(
                    PS[:, os * 512:(os + 1) * 512], onesb[0:1, :],
                    brow[0:1, 1024 + os * 512:1024 + (os + 1) * 512],
                    start=False, stop=True),
                  inc=[(s_pe, 1)] if os == 1 else ())
            wo_done = (s_pe, s_pe.n)
            E(V, lambda xs=xs, t=t: nc.vector.tensor_scalar(
                xn32[:, :], xs, stats[:, 2 * t:2 * t + 1],
                stats[:, 2 * t + 1:2 * t + 2], ALU.subtract, ALU.mult),
              wait=[xB_ready, (s_dve, s_dve.n)], inc=[(s_dve, 1)])
            E(V, lambda: nc.vector.tensor_add(zbuf[:, :], PS[:, 0:1024],
                                              xn32[:, :]),
              wait=[wo_done, (s_dve, s_dve.n)], inc=[(s_dve, 1)])
            bz_evac = (s_dve, s_dve.n)
            r_ok = _ln0(zbuf)
            E(V, lambda: nc.vector.tensor_scalar(
                abt[:, :], zbuf[:, :], mean0, rstd0, ALU.subtract, ALU.mult),
              wait=[r_ok, (s_dve, s_dve.n)], inc=[(s_dve, 1)])
            ab_ok = (s_dve, s_dve.n)
            ab_evac = {}
            for c in range(8):
                pb = (2048 if c % 2 == 0 else 3584) + ((c // 2) % 4) * 128
                w_d = [ab_ok] if c == 0 else []
                if c == 0 and t > 0:
                    w_d.append(prev_ff1_in)
                if c >= 2:
                    w_d.append(ab_evac[c - 2])
                E(P, lambda c=c, pb=pb: nc.tensor.transpose(
                    PS[:, pb:pb + 128], abt[:, c * 128:(c + 1) * 128],
                    identf[:, :]), wait=w_d, inc=[(s_pe, 1)])
                tp_ok = (s_pe, s_pe.n)
                E(V, lambda c=c, pb=pb: nc.vector.tensor_copy(
                    ffA[:, c * 128:(c + 1) * 128], PS[:, pb:pb + 128]),
                  wait=[tp_ok], inc=[(s_dve, 1)])
                ab_evac[c] = (s_dve, s_dve.n)
            prev_ff1_in = ab_evac[7]
            ff1_evac = {}
            for c in range(8):
                pb = 1024 + (c % 2) * 512 + ((c // 2) % 4) * 128
                for kc in range(8):
                    w_d = []
                    if kc == 0:
                        w_d = [ab_evac[7]]
                        if c == 0 and ff1_last is not None:
                            w_d.append(ff1_last)
                        if c >= 2:
                            w_d.append(ff1_evac[c - 2])
                    E(P, lambda c=c, kc=kc, pb=pb: nc.tensor.matmul(
                        PS[:, pb:pb + 128],
                        wB[:, kc * 1024 + c * 128:kc * 1024 + (c + 1) * 128],
                        ffA[:, kc * 128:(kc + 1) * 128],
                        start=(kc == 0), stop=(kc == 7)),
                      wait=w_d, inc=[(s_pe, 1)] if kc == 7 else ())
                w1_ok = (s_pe, s_pe.n)
                E(V, lambda c=c, pb=pb: nc.vector.tensor_scalar(
                    ffB[:, c * 128:(c + 1) * 128], PS[:, pb:pb + 128],
                    bias_pp[:, 16 + c:17 + c], None, ALU.add),
                  wait=[w1_ok], inc=[(s_dve, 1)])
                ff1_evac[c] = (s_dve, s_dve.n)
            ff1_last = ff1_evac[7]
            for os in range(2):
                pb = 2560 + os * 512
                for c in range(8):
                    w_d = []
                    if c == 0:
                        w_d = [ff1_evac[7]]
                        if os == 0 and s_spill is not None:
                            w_d.append(s_spill)
                    E(P, lambda os=os, c=c, pb=pb: nc.tensor.matmul(
                        PS[:, pb:pb + 512], ffB[:, c * 128:(c + 1) * 128],
                        wC[:, c * 1024 + os * 512:c * 1024 + (os + 1) * 512],
                        start=(c == 0), stop=False), wait=w_d)
                E(P, lambda os=os, pb=pb: nc.tensor.matmul(
                    PS[:, pb:pb + 512], onesb[0:1, :],
                    brow[0:1, 2048 + os * 512:2048 + (os + 1) * 512],
                    start=False, stop=True),
                  inc=[(s_pe, 1)] if os == 1 else ())
            w2_done = (s_pe, s_pe.n)
            E(V, lambda: nc.vector.tensor_add(zbuf[:, :], PS[:, 2560:3584],
                                              abt[:, :]),
              wait=[w2_done, (s_dve, s_dve.n)], inc=[(s_dve, 1)])
            r_ok = _ln0(zbuf)
            E(V, lambda: nc.vector.tensor_scalar(
                xn32[:, :], zbuf[:, :], mean0, rstd0, ALU.subtract, ALU.mult),
              wait=[r_ok, (s_dve, s_dve.n)], inc=[(s_dve, 1)])
            s_ok = (s_dve, s_dve.n)
            E(G, lambda t=t: nc.gpsimd.dma_start(
                out=s_d.ap()[t * 128:(t + 1) * 128, :], in_=xn32[:, :]),
              wait=[s_ok], inc=[(s_do, 16)])
            s_spill = (s_do, s_do.n)
            bx_done[t] = s_spill
        phaseB_done_pe = (s_pe, s_pe.n)
        phaseB_done_do = (s_do, s_do.n)

        for j in range(4):
            E(V, lambda j=j: nc.vector.memset(acc[:, j * 4096:(j + 1) * 4096],
                                              0.0), inc=[(s_dve, 1)])
        acc_init = (s_dve, s_dve.n)

        pool_hist.clear()
        cast_done = {}
        gu_free = {}

        def _expw(e, par, first_wait=None, dn_wait=None):
            for j in range(24):
                if j < 16:
                    dc, half = j // 2, j % 2
                    src = din["gu"].ap()[e, dc * 128:(dc + 1) * 128,
                                         half * 512:(half + 1) * 512]
                    tgt, toff = guW[par], dc * 1024 + half * 512
                else:
                    jj = j - 16
                    dc, half = jj // 2, jj % 2
                    src = din["dn"].ap()[e, dc * 128:(dc + 1) * 128,
                                         half * 512:(half + 1) * 512]
                    tgt, toff = dnW[par], dc * 1024 + half * 512
                slot = j % 2
                sw = s_w0 if slot == 0 else s_w1
                w_d = []
                if j == 0 and first_wait is not None:
                    w_d.append(first_wait)
                if len(pool_hist) >= 2:
                    w_d.append((s_pool, pool_hist[-2]))
                E(X, lambda d=wstage[:, slot * 512:(slot + 1) * 512], s=src:
                  nc.sync.dma_start(out=d, in_=s), wait=w_d, inc=[(sw, 16)])
                dma_ok = (sw, sw.n)
                cw = [dma_ok]
                if j == 0 and e >= 2:
                    cw.append(gu_free[e - 2])
                if j == 16 and dn_wait is not None:
                    cw.append(dn_wait)
                E(G, lambda tgt=tgt, toff=toff, slot=slot:
                  nc.gpsimd.tensor_copy(
                      tgt[:, toff:toff + 512],
                      wstage[:, slot * 512:(slot + 1) * 512]),
                  wait=cw, inc=[(s_pool, 1)])
                pool_hist.append(s_pool.n)
            cast_done[e] = (s_pool, s_pool.n)

        _expw(0, 0, first_wait=phaseB_done_pe)
        _expw(1, 1)

        silu_hist = {}
        tt1_hist = {}
        stt_hist = {}
        down_end = {}
        for e in range(NE):
            par = e % 2
            h_e = UH[:, par * 8192:(par + 1) * 8192]
            for it in range(16):
                ns, ip = it // 4, it % 4
                q = it % 2
                pg = q * 1024
                pu = q * 1024 + 512
                for dc in range(8):
                    w_d = []
                    if dc == 0:
                        w_d = [cast_done[e]]
                        if it >= 2:
                            w_d += [silu_hist[(e, it - 2)],
                                    tt1_hist[(e, it - 2)]]
                        elif e > 0:
                            w_d += [silu_hist[(e - 1, 14 + q)],
                                    tt1_hist[(e - 1, 14 + q)]]
                        else:
                            w_d.append(phaseB_done_pe)
                    E(P, lambda par=par, dc=dc, ip=ip, ns=ns, pg=pg:
                      nc.tensor.matmul(
                          PS[:, pg:pg + 512],
                          guW[par][:, dc * 1024 + ip * 128:
                                   dc * 1024 + (ip + 1) * 128],
                          xn_fm[:, dc * 2048 + ns * 512:
                                dc * 2048 + (ns + 1) * 512],
                          start=(dc == 0), stop=(dc == 7)),
                      wait=w_d, inc=[(s_pe, 1)] if dc == 7 else ())
                g_ok = (s_pe, s_pe.n)
                for dc in range(8):
                    E(P, lambda par=par, dc=dc, ip=ip, ns=ns, pu=pu:
                      nc.tensor.matmul(
                          PS[:, pu:pu + 512],
                          guW[par][:, dc * 1024 + 512 + ip * 128:
                                   dc * 1024 + 512 + (ip + 1) * 128],
                          xn_fm[:, dc * 2048 + ns * 512:
                                dc * 2048 + (ns + 1) * 512],
                          start=(dc == 0), stop=(dc == 7)),
                      inc=[(s_pe, 1)] if dc == 7 else ())
                u_ok = (s_pe, s_pe.n)
                E(A, lambda q=q, pg=pg: nc.scalar.activation(
                    abt[:, q * 512:(q + 1) * 512], PS[:, pg:pg + 512],
                    AF.Silu), wait=[g_ok], inc=[(s_act, 1)])
                silu_hist[(e, it)] = (s_act, s_act.n)
                w_t = [u_ok, silu_hist[(e, it)]]
                if e == 0 and it == 0:
                    w_t += [acc_init, attA_done_dve]
                E(V, lambda h_e=h_e, q=q, ip=ip, ns=ns, pu=pu:
                  nc.vector.tensor_mul(
                      h_e[:, ip * 2048 + ns * 512:ip * 2048 + (ns + 1) * 512],
                      abt[:, q * 512:(q + 1) * 512], PS[:, pu:pu + 512]),
                  wait=w_t, inc=[(s_dve, 1)])
                tt1_hist[(e, it)] = (s_dve, s_dve.n)
            h_ready = (s_dve, s_dve.n)
            gu_free[e] = (s_pe, s_pe.n)
            for tt in range(NT):
                q2 = tt % 2
                pd = 2048 + q2 * 1024
                for os in range(2):
                    for ip in range(4):
                        w_d = []
                        if os == 0 and ip == 0:
                            w_d = [h_ready]
                            if (e, tt - 2) in stt_hist:
                                w_d.append(stt_hist[(e, tt - 2)])
                            elif e > 0:
                                w_d.append(stt_hist[(e - 1, 14 + q2)])
                        E(P, lambda tt=tt, os=os, ip=ip, pd=pd, par=par:
                          nc.tensor.matmul(
                              PS[:, pd + os * 512:pd + (os + 1) * 512],
                              UH[:, par * 8192 + ip * 2048 + tt * 128:
                                 par * 8192 + ip * 2048 + (tt + 1) * 128],
                              dnW[par][:, ip * 1024 + os * 512:
                                       ip * 1024 + (os + 1) * 512],
                              start=(ip == 0), stop=(ip == 3)),
                          wait=w_d,
                          inc=[(s_pe, 1)] if (os == 1 and ip == 3) else ())
                d_ok = (s_pe, s_pe.n)
                E(V, lambda e=e, tt=tt, pd=pd:
                  nc.vector.scalar_tensor_tensor(
                      acc[:, tt * 1024:(tt + 1) * 1024], PS[:, pd:pd + 1024],
                      combine[:, tt * 32 + e:tt * 32 + e + 1],
                      acc[:, tt * 1024:(tt + 1) * 1024], ALU.mult, ALU.add),
                  wait=[d_ok], inc=[(s_dve, 1)])
                stt_hist[(e, tt)] = (s_dve, s_dve.n)
            down_end[e] = (s_dve, s_dve.n)
            if e + 2 < NE:
                _expw(e + 2, par, dn_wait=(s_pe, s_pe.n))
        pe_all_exp = (s_pe, s_pe.n)

        pool_hist.clear()

        def _shload(key, tgt, first_wait=()):
            for cc in range(16):
                dc, half = cc // 2, cc % 2
                slot = cc % 2
                w_d = []
                if cc == 0:
                    w_d.extend(first_wait)
                if len(pool_hist) >= 2:
                    w_d.append((s_pool, pool_hist[-2]))
                src = din[key].ap()[dc * 128:(dc + 1) * 128,
                                    half * 512:(half + 1) * 512]
                sw = s_w0 if slot == 0 else s_w1
                E(X, lambda d=wstage[:, slot * 512:(slot + 1) * 512], s=src:
                  nc.sync.dma_start(out=d, in_=s), wait=w_d, inc=[(sw, 16)])
                dma_ok = (sw, sw.n)
                E(G, lambda tgt=tgt, dc=dc, half=half, slot=slot:
                  nc.gpsimd.tensor_copy(
                      tgt[:, dc * 1024 + half * 512:
                          dc * 1024 + (half + 1) * 512],
                      wstage[:, slot * 512:(slot + 1) * 512]),
                  wait=[dma_ok], inc=[(s_pool, 1)])
                pool_hist.append(s_pool.n)

        _shload("sg", guW[0], first_wait=[gu_free[NE - 2],
                                          cast_done[NE - 1]])
        _shload("su", guW[1], first_wait=[gu_free[NE - 1]])
        _shload("sd", sdW, first_wait=[pe_all_exp])
        sh_cast = (s_pool, s_pool.n)

        sh_h = UH
        sh_silu = {}
        sh_tt1 = {}
        for it in range(32):
            ns, ip = it // 8, it % 8
            q = it % 2
            pg = q * 1024
            pu = q * 1024 + 512
            for dc in range(8):
                w_d = []
                if dc == 0:
                    w_d = [sh_cast]
                    if it >= 2:
                        w_d += [sh_silu[it - 2], sh_tt1[it - 2]]
                    else:
                        w_d += [down_end[NE - 1],
                                silu_hist[(NE - 1, 14 + q)]]
                E(P, lambda dc=dc, ip=ip, ns=ns, pg=pg: nc.tensor.matmul(
                    PS[:, pg:pg + 512],
                    guW[0][:, dc * 1024 + ip * 128:dc * 1024 + (ip + 1) * 128],
                    xn_fm[:, dc * 2048 + ns * 512:dc * 2048 + (ns + 1) * 512],
                    start=(dc == 0), stop=(dc == 7)),
                  wait=w_d, inc=[(s_pe, 1)] if dc == 7 else ())
            g_ok = (s_pe, s_pe.n)
            for dc in range(8):
                E(P, lambda dc=dc, ip=ip, ns=ns, pu=pu: nc.tensor.matmul(
                    PS[:, pu:pu + 512],
                    guW[1][:, dc * 1024 + ip * 128:dc * 1024 + (ip + 1) * 128],
                    xn_fm[:, dc * 2048 + ns * 512:dc * 2048 + (ns + 1) * 512],
                    start=(dc == 0), stop=(dc == 7)),
                  inc=[(s_pe, 1)] if dc == 7 else ())
            u_ok = (s_pe, s_pe.n)
            E(A, lambda q=q, pg=pg: nc.scalar.activation(
                abt[:, q * 512:(q + 1) * 512], PS[:, pg:pg + 512], AF.Silu),
              wait=[g_ok], inc=[(s_act, 1)])
            sh_silu[it] = (s_act, s_act.n)
            w_t = [u_ok, sh_silu[it]]
            if it < 2:
                w_t.append(down_end[NE - 1])
            E(V, lambda q=q, ip=ip, ns=ns, pu=pu: nc.vector.tensor_mul(
                sh_h[:, ip * 2048 + ns * 512:ip * 2048 + (ns + 1) * 512],
                abt[:, q * 512:(q + 1) * 512], PS[:, pu:pu + 512]),
              wait=w_t, inc=[(s_dve, 1)])
            sh_tt1[it] = (s_dve, s_dve.n)
        sh_h_ready = (s_dve, s_dve.n)

        fin_z = {}
        fin_out = {}
        g_ok2 = None
        for tt in range(NT):
            q2 = tt % 2
            pd = 2048 + q2 * 1024
            for os in range(2):
                for ip in range(8):
                    w_d = []
                    if os == 0 and ip == 0:
                        w_d = [sh_h_ready]
                        if (tt - 2) in fin_z:
                            w_d.append(fin_z[tt - 2])
                        else:
                            w_d.append(stt_hist[(NE - 1, 14 + q2)])
                    E(P, lambda tt=tt, os=os, ip=ip, pd=pd: nc.tensor.matmul(
                        PS[:, pd + os * 512:pd + (os + 1) * 512],
                        sh_h[:, ip * 2048 + tt * 128:ip * 2048 + (tt + 1) * 128],
                        sdW[:, ip * 1024 + os * 512:ip * 1024 + (os + 1) * 512],
                        start=(ip == 0), stop=(ip == 7)),
                      wait=w_d,
                      inc=[(s_pe, 1)] if (os == 1 and ip == 7) else ())
            d_ok = (s_pe, s_pe.n)
            w_x = [phaseB_done_do] if tt < 1 else [fin_out[tt - 1]]
            E(G, lambda tt=tt: nc.gpsimd.dma_start(
                out=x_sb[:, :],
                in_=din["x"].ap()[tt * 128:(tt + 1) * 128, :]),
              wait=w_x, inc=[(s_x, 16)])
            x_ok = (s_x, s_x.n)
            E(G, lambda tt=tt: nc.gpsimd.dma_start(
                out=xn32[:, :], in_=s_d.ap()[tt * 128:(tt + 1) * 128, :]),
              wait=[(s_dve, s_dve.n)], inc=[(s_do, 16)])
            xsr = (s_do, s_do.n)
            w_z = [d_ok, (s_dve, s_dve.n)]
            if g_ok2 is not None:
                w_z.append(g_ok2)
            E(V, lambda tt=tt, pd=pd: nc.vector.tensor_add(
                zbuf[:, :], PS[:, pd:pd + 1024],
                acc[:, tt * 1024:(tt + 1) * 1024]), wait=w_z,
              inc=[(s_dve, 1)])
            fin_z[tt] = (s_dve, s_dve.n)
            E(A, lambda: nc.scalar.activation(abt[:, :], zbuf[:, :], AF.Silu),
              wait=[fin_z[tt]], inc=[(s_act, 1)])
            g_ok2 = (s_act, s_act.n)
            mul_w = [g_ok2, xsr, (s_dve, s_dve.n)]
            if (tt - 1) in fin_out:
                mul_w.append(fin_out[tt - 1])
            E(V, lambda: nc.vector.tensor_mul(
                outt[:, 0:1024], abt[:, :], xn32[:, :]),
              wait=mul_w, inc=[(s_dve, 1)])
            E(V, lambda: nc.vector.tensor_add(
                outt[:, 0:1024], outt[:, 0:1024], x_sb[:, 0:1024]),
              wait=[x_ok, (s_dve, s_dve.n)], inc=[(s_dve, 1)])
            o_ok = (s_dve, s_dve.n)
            E(G, lambda tt=tt: nc.gpsimd.dma_start(
                out=out_d.ap()[tt * 128:(tt + 1) * 128, :],
                in_=outt[:, 0:1024]),
              wait=[o_ok], inc=[(s_do, 16)])
            fin_out[tt] = (s_do, s_do.n)
        if DEBUG:
            E(G, lambda: nc.gpsimd.dma_start(
                out=dbg_acc.ap()[:, :], in_=acc[:, :]),
              wait=[(s_dve, s_dve.n)], inc=[(s_do, 16)])
        final_do = (s_do, s_do.n)
        b.W(X, final_do)
        b.W(G, final_do)

        with nc.Block() as blk:
            def _mk(eng_name):
                def run(eng):
                    for fn, waits, incs in b.q[eng_name]:
                        for h, v in waits:
                            eng.wait_ge(h, v)
                        if fn is None:
                            continue
                        inst = fn()
                        for h, k in incs:
                            inst.then_inc(h, k)
                return run
            blk.sync(_mk("sync"))
            blk.vector(_mk("vector"))
            blk.scalar(_mk("scalar"))
            blk.tensor(_mk("tensor"))
            blk.gpsimd(_mk("gpsimd"))

    return nc


def kernel(**inputs):
    global _PROG
    p = _host_prep(inputs)
    if _PROG is None:
        _PROG = _build()
    nc = _PROG
    shared = {k: v for k, v in p.items() if k != "x_all"}
    in_maps = []
    for c in range(NCORE):
        mm = dict(shared)
        mm["x"] = p["x_all"][c]
        in_maps.append(mm)
    res = run_bass_kernel_spmd(nc, in_maps, core_ids=list(range(NCORE)))
    return np.stack([np.asarray(r["out"]) for r in res.results], axis=0)


# version 05
# speedup vs baseline: 1.0680x; 1.0680x over previous
import contextlib
import math

import numpy as np
import ml_dtypes

import concourse.bass as bass
import concourse.mybir as mybir
from concourse.ap import AP
from concourse.bass_utils import run_bass_kernel_spmd

F32 = mybir.dt.float32
BF16 = mybir.dt.bfloat16
ALU = mybir.AluOpType
AF = mybir.ActivationFunctionType
AX = mybir.AxisListType

NCORE = 8
T = 2048
D = 1024
NT = 16
NM = 16
NE = 32
EXP_SHIFT = -30.0


def _bucket_1d():
    delta = np.arange(-2047, 2048, dtype=np.int32)
    n = -delta
    nb = 16
    ret = (n < 0).astype(np.int32) * nb
    n = np.abs(n)
    max_exact = 8
    is_small = n < max_exact
    ratio = np.log(np.maximum(n, 1).astype(np.float32)
                   / np.float32(max_exact)) / np.float32(math.log(128 / 8))
    vl = max_exact + (ratio * np.float32(nb - max_exact)).astype(np.int32)
    vl = np.minimum(vl, nb - 1)
    return ret + np.where(is_small, n, vl)


def _host_prep(inputs):
    ct = np.ascontiguousarray
    p = {}
    p["x_all"] = ct(np.asarray(inputs["L0_attn_x"], dtype=np.float32))
    for nm, key in [("wq", "L0_attn_Wq"), ("wk", "L0_attn_Wk"),
                    ("wv", "L0_attn_Wv"), ("wo", "L0_attn_Wo"),
                    ("w1", "L0_attn_W1"), ("w2", "L0_attn_W2"),
                    ("sg", "L0_ffn_shared_gate_w"), ("su", "L0_ffn_shared_up_w"),
                    ("sd", "L0_ffn_shared_down_w")]:
        p[nm] = ct(np.asarray(inputs[key], dtype=np.float32).T)
    p["gw"] = ct(np.asarray(inputs["L0_ffn_gate_weight"], np.float32).T)
    p["gu"] = ct(np.transpose(
        np.asarray(inputs["L0_ffn_experts_gate_up"], np.float32), (0, 2, 1)))
    p["dn"] = ct(np.transpose(
        np.asarray(inputs["L0_ffn_experts_down"], np.float32), (0, 2, 1)))
    for nm, key in [("bq", "L0_attn_bq"), ("bk", "L0_attn_bk"),
                    ("bv", "L0_attn_bv"), ("bo", "L0_attn_bo"),
                    ("b1", "L0_attn_b1"), ("b2", "L0_attn_b2")]:
        p[nm] = ct(np.asarray(inputs[key], np.float32).reshape(1, D))
    p["ebias"] = ct(np.asarray(
        inputs["L0_ffn_e_score_correction_bias"], np.float32).reshape(1, NE))

    buckets = _bucket_1d()
    rb = np.asarray(inputs["L0_attn_rel_bias"], np.float32)
    band = rb[buckets, :]
    band = np.concatenate([band, np.zeros((1, 16), band.dtype)], 0)
    idx = (16 * np.arange(256)[None, :] + np.arange(31)[:, None])
    idx = np.minimum(idx, 4095)
    hv = band[idx, :]
    p["hv"] = ct(np.transpose(hv, (2, 0, 1)).astype(np.float32))

    bf = ml_dtypes.bfloat16
    p["identf"] = np.eye(128, dtype=np.float32)
    p["identb"] = np.eye(128, dtype=bf)
    p["revb"] = np.eye(128, dtype=np.float32)[::-1].astype(bf)
    p["onesb"] = np.ones((1, 128), dtype=bf)
    return p


class _Sem:
    __slots__ = ("h", "n")


class _B:
    def __init__(self, nc, st):
        self.nc = nc
        self.st = st
        self.q = {"sync": [], "vector": [], "scalar": [], "tensor": [],
                  "gpsimd": []}

    def sem(self, name):
        s = _Sem()
        s.h = self.st.enter_context(self.nc.semaphore(name))
        s.n = 0
        return s

    def E(self, eng, fn, wait=(), inc=()):
        for s, k in inc:
            s.n += k
        self.q[eng].append((fn, tuple((s.h, v) for s, v in wait),
                            tuple((s.h, k) for s, k in inc)))

    def W(self, eng, *waits):
        self.q[eng].append((None, tuple((s.h, v) for s, v in waits), ()))


_PROG = None
_DBG = {}
DEBUG = False


def _build():
    nc = bass.Bass("TRN2", target_bir_lowering=False,
                   dynamic_dma_scratch_size=16384)
    din = {}
    for name, shape, dt in [
        ("x", [T, D], F32), ("wq", [D, D], F32), ("wk", [D, D], F32),
        ("wv", [D, D], F32), ("wo", [D, D], F32), ("w1", [D, D], F32),
        ("w2", [D, D], F32), ("sg", [D, D], F32), ("su", [D, D], F32),
        ("sd", [D, D], F32), ("gw", [D, NE], F32),
        ("gu", [NE, D, D], F32), ("dn", [NE, 512, D], F32),
        ("bq", [1, D], F32), ("bk", [1, D], F32), ("bv", [1, D], F32),
        ("bo", [1, D], F32), ("b1", [1, D], F32), ("b2", [1, D], F32),
        ("ebias", [1, NE], F32), ("hv", [16, 31, 256], F32),
        ("identf", [128, 128], F32), ("identb", [128, 128], BF16),
        ("revb", [128, 128], BF16), ("onesb", [1, 128], BF16),
    ]:
        din[name] = nc.dram_tensor(name, shape, dt, kind="ExternalInput")
    out_d = nc.dram_tensor("out", [T, D], F32, kind="ExternalOutput")
    hvb_d = nc.dram_tensor("hvb", [16 * 31 * 256], BF16, kind="Internal")
    s_d = nc.dram_tensor("sdram", [T, D], F32,
                         kind="ExternalOutput" if DEBUG else "Internal")
    if DEBUG:
        dbg_xn = nc.dram_tensor("dbg_xn", [128, 16384], F32, kind="ExternalOutput")
        dbg_sc = nc.dram_tensor("dbg_sc", [128, 1024], F32, kind="ExternalOutput")
        dbg_acc = nc.dram_tensor("dbg_acc", [128, 16384], F32, kind="ExternalOutput")
        dbg_att = nc.dram_tensor("dbg_att", [128, 16384], F32, kind="ExternalOutput")

    st = contextlib.ExitStack()
    with st:
        S = lambda *a: st.enter_context(nc.sbuf_tensor(*a))
        c8 = S("c8", [128, 8], F32)
        identf = S("identf_s", [128, 128], F32)
        revb = S("revb_s", [128, 128], BF16)
        onesb = S("onesb_s", [1, 128], BF16)
        onesf = S("onesf_s", [1, 128], F32)
        bias_pp = S("bias_pp", [128, 24], F32)
        ebias_bc = S("ebias_bc", [128, NE], F32)
        stats = S("stats", [128, 32], F32)
        r_sb = S("r_sb", [128, 160], F32)
        combine = S("combine", [128, 512], F32)
        xn_fm = S("xn_fm", [128, 16384], BF16)
        x_sb = S("x_sb", [128, 1024], F32)
        wstage = S("wstage", [128, 1024], F32)
        xn32 = S("xn32", [128, 1024], F32)
        zbuf = S("zbuf", [128, 1024], F32)
        UH = S("UH", [128, 16384], BF16)
        abt = S("abt", [128, 1024], F32)
        outt = S("outt", [128, 1024], F32)
        UW = S("UW", [128, 12288], F32)
        ACCU = S("ACCU", [128, 16384], F32)

        bf16 = lambda ap: ap.bitcast(BF16)
        logits = ACCU[:, 0:2048][:, 0:512]
        E2 = bf16(ACCU[:, 2048:4096])
        qm = bf16(ACCU[:, 4096:4608])
        kmA = bf16(ACCU[:, 4608:5120])
        kmB = bf16(ACCU[:, 5120:5632])
        vm = bf16(ACCU[:, 5632:6144])
        diag2 = bf16(ACCU[:, 6144:6272])
        hst2 = bf16(ACCU[:, 6272:6400])
        partials = ACCU[:, 6400:6408]
        PT = bf16(ACCU[:, 6656:10752])
        UBT = ACCU[:, 10752:14848]
        scores = ACCU[:, 512:1024]
        ffb = bf16(ACCU[:, 1024:2048])
        gw_sb = ACCU[:, 2048:2304]
        brow_st = ACCU[0:8, 6408:6536]
        mvp = ACCU[:, 6536:6552]
        hstB = bf16(ACCU[:, 15616:16384])
        wA = bf16(UW[:, 0:4096])
        wB = bf16(UW[:, 4096:8192])
        wC = bf16(UW[:, 8192:12288])
        guW = [bf16(UW[:, 0:4096]), bf16(UW[:, 4096:8192])]
        dnW = [bf16(UW[:, 8192:10240]), bf16(UW[:, 10240:12288])]
        sdW = bf16(UW[:, 8192:12288])
        brow = ACCU[0:1, 14848:16384].bitcast(BF16)
        ffA = ffb[:, 0:1024]
        ffB = ffb[:, 1024:2048]
        acc = ACCU

        PS = st.enter_context(nc.psum_tensor("PSA", [128, 4096], F32))
        _DBG.update(dict(UH=UH, ACCU=ACCU, UW=UW, xn_fm=xn_fm, UBT=UBT,
                         scores=scores, combine=combine, stats=stats))

        b = _B(nc, st)
        s_pro = b.sem("s_pro")
        s_w0 = b.sem("s_w0")
        s_w1 = b.sem("s_w1")
        s_hk0 = b.sem("s_hk0")
        s_hk1 = b.sem("s_hk1")
        s_kmb = b.sem("s_kmb")
        s_row = b.sem("s_row")
        s_bst = b.sem("s_bst")
        s_x = b.sem("s_x")
        s_do = b.sem("s_do")
        s_pe = b.sem("s_pe")
        s_act = b.sem("s_act")
        s_dve = b.sem("s_dve")
        s_pool = b.sem("s_pool")
        E = b.E
        V, A, P, X, G = "vector", "scalar", "tensor", "sync", "gpsimd"

        cvals = [EXP_SHIFT, 1e-5, -0.5, 1e-20, 2.5, -1e30, 1e30, 0.0]
        for i, v in enumerate(cvals):
            E(V, lambda i=i, v=v: nc.vector.memset(c8[:, i:i + 1], v))
        E(V, lambda: nc.vector.memset(onesf[:, :], 1.0), inc=[(s_dve, 1)])
        consts_ok = (s_dve, s_dve.n)
        c_shift = c8[:, 0:1]
        c_eps = c8[:, 1:2]
        c_mhalf = c8[:, 2:3]
        c_tiny = c8[:, 3:4]
        c_25 = c8[:, 4:5]
        c_msent = c8[:, 5:6]

        def _dma(eng, sem, dst, src, wait=()):
            E(eng, lambda d=dst, s=src: getattr(nc, eng).dma_start(out=d, in_=s),
              wait=wait, inc=[(sem, 16)])

        for dst, src in [(identf[:, :], din["identf"].ap()),
                         (revb[:, :], din["revb"].ap()),
                         (onesb[:, :], din["onesb"].ap()),
                         (ebias_bc[0:1, :], din["ebias"].ap())]:
            _dma(X, s_pro, dst, src)
        gw_src = AP(din["gw"], 0, [[NE, 128], [128 * NE, 8], [1, NE]])
        gw_dst = gw_sb[:, 0:256].rearrange("p (c e) -> p c e", e=NE)
        _dma(X, s_pro, gw_dst, gw_src)
        hv_flat = din["hv"].ap().rearrange("a b c -> (a b c)")
        hv_f = UH.bitcast(F32)[:, 0:992]
        _dma(X, s_pro, hv_f, hv_flat.rearrange("(p k) -> p k", p=128))
        n_pro_dw = s_pro.n
        hv_b = UH[:, 2048:3040]
        E(G, lambda: nc.gpsimd.tensor_copy(hv_b, hv_f),
          wait=[(s_pro, n_pro_dw)], inc=[(s_pool, 1)])
        E(X, lambda: nc.sync.dma_start(
            out=hvb_d.ap().rearrange("(p k) -> p k", p=128), in_=hv_b),
          wait=[(s_pool, s_pool.n)], inc=[(s_pro, 16)])
        hv_done = (s_pro, s_pro.n)
        xsb_free = hv_done

        for j, nm in enumerate(["bv", "bo", "b2"]):
            _dma(X, s_row, UBT[0:1, j * 1024:(j + 1) * 1024], din[nm].ap())
        rows_dw = (s_row, s_row.n)
        for j in range(3):
            E(V, lambda j=j: nc.vector.tensor_copy(
                brow[0:1, j * 1024:(j + 1) * 1024],
                UBT[0:1, j * 1024:(j + 1) * 1024]),
              wait=[rows_dw] if j == 0 else (), inc=[(s_dve, 1)])
        brow_done = (s_dve, s_dve.n)

        ident_ok = (s_pro, n_pro_dw)
        for j, nm in enumerate(["bq", "bk", "b1"]):
            w_d = [] if j == 0 else [(s_pe, s_pe.n)]
            _dma(X, s_bst, brow_st[:, :],
                 din[nm].ap().rearrange("a (c p) -> (a c) p", p=128), wait=w_d)
            w_ld = (s_bst, s_bst.n)
            E(P, lambda j=j: nc.tensor.transpose(
                PS[:, 3584 + j * 128:3584 + j * 128 + 8], brow_st[:, :],
                identf[0:8, 0:8]),
              wait=[w_ld, ident_ok, (s_dve, s_dve.n)], inc=[(s_pe, 1)])
            pe_t = (s_pe, s_pe.n)
            E(V, lambda j=j: nc.vector.tensor_copy(
                bias_pp[:, j * 8:(j + 1) * 8],
                PS[:, 3584 + j * 128:3584 + j * 128 + 8]),
              wait=[pe_t], inc=[(s_dve, 1)])
        E(P, lambda: nc.tensor.matmul(PS[:, 3072:3104], onesf[0:1, :],
                                      ebias_bc[0:1, :], start=True, stop=True),
          wait=[(s_pro, n_pro_dw), consts_ok], inc=[(s_pe, 1)])
        E(V, lambda: nc.vector.tensor_copy(ebias_bc[:, :], PS[:, 3072:3104]),
          wait=[(s_pe, s_pe.n)], inc=[(s_dve, 1)])
        pro_psum_evac = (s_dve, s_dve.n)

        pool_hist = []

        def _wload(key, tgt, first_wait=None, cast_wait=None, src3=None):
            for cc in range(16):
                dc, half = cc // 2, cc % 2
                slot = cc % 2
                w_d = []
                if first_wait is not None and cc == 0:
                    w_d.append(first_wait)
                if len(pool_hist) >= 2:
                    w_d.append((s_pool, pool_hist[-2]))
                if src3 is None:
                    src = din[key].ap()[dc * 128:(dc + 1) * 128,
                                        half * 512:(half + 1) * 512]
                else:
                    src = src3[dc * 128:(dc + 1) * 128,
                               half * 512:(half + 1) * 512]
                sw = s_w0 if slot == 0 else s_w1
                E(X, lambda d=wstage[:, slot * 512:(slot + 1) * 512], s=src:
                  nc.sync.dma_start(out=d, in_=s), wait=w_d, inc=[(sw, 16)])
                dma_ok = (sw, sw.n)
                cw = [dma_ok]
                if cast_wait is not None and cc == 0:
                    cw.append(cast_wait)
                E(G, lambda tgt=tgt, dc=dc, half=half, slot=slot:
                  nc.gpsimd.tensor_copy(
                      tgt[:, dc * 1024 + half * 512:dc * 1024 + (half + 1) * 512],
                      wstage[:, slot * 512:(slot + 1) * 512]),
                  wait=cw, inc=[(s_pool, 1)])
                pool_hist.append(s_pool.n)

        _wload("wq", wA)
        _wload("wk", wB)
        _wload("wv", wC)
        wqkv_pool = (s_pool, s_pool.n)

        xt_consumed = {}
        prev_evac = pro_psum_evac
        prev_tr7 = None
        prev_ln = None
        for t in range(NT):
            waits = [xsb_free]
            if t >= 1:
                waits.append(xt_consumed[t - 1])
            E(G, lambda t=t: nc.gpsimd.dma_start(
                out=x_sb[:, :],
                in_=din["x"].ap()[t * 128:(t + 1) * 128, :]),
              wait=waits, inc=[(s_x, 16)])
            x_ready = (s_x, s_x.n)
            xs = x_sb[:, 0:1024]
            E(V, lambda xs=xs: nc.vector.bn_stats(mvp[:, 0:6], xs[:, 0:512]),
              wait=[x_ready, (s_dve, s_dve.n)], inc=[(s_dve, 1)])
            E(V, lambda xs=xs: nc.vector.bn_stats(mvp[:, 6:12], xs[:, 512:1024]),
              wait=[(s_dve, s_dve.n)], inc=[(s_dve, 1)])
            E(V, lambda: nc.vector.bn_aggr(mvp[:, 12:14], mvp[:, 0:12]),
              wait=[(s_dve, s_dve.n)], inc=[(s_dve, 1)])
            ts_w = [(s_dve, s_dve.n), consts_ok]
            if prev_ln is not None:
                ts_w.append(prev_ln)
            E(V, lambda: nc.vector.tensor_scalar(
                mvp[:, 14:15], mvp[:, 13:14], c_eps, None, ALU.add),
              wait=ts_w, inc=[(s_dve, 1)])
            v_mv = (s_dve, s_dve.n)
            E(A, lambda: nc.scalar.activation(mvp[:, 15:16], mvp[:, 14:15],
                                              AF.Ln),
              wait=[v_mv], inc=[(s_act, 1)])
            prev_ln = (s_act, s_act.n)
            E(A, lambda t=t: nc.scalar.activation(
                stats[:, 2 * t + 1:2 * t + 2], mvp[:, 15:16], AF.Exp,
                scale=c_mhalf), wait=[(s_act, s_act.n)], inc=[(s_act, 1)])
            rstd_done = (s_act, s_act.n)
            E(V, lambda t=t: nc.vector.tensor_copy(
                stats[:, 2 * t:2 * t + 1], mvp[:, 12:13]),
              wait=[(s_dve, s_dve.n)], inc=[(s_dve, 1)])
            tsp_w = [rstd_done, (s_dve, s_dve.n)]
            if prev_tr7 is not None:
                tsp_w.append(prev_tr7)
            E(V, lambda xs=xs, t=t: nc.vector.tensor_scalar(
                xn32[:, :], xs, stats[:, 2 * t:2 * t + 1],
                stats[:, 2 * t + 1:2 * t + 2], ALU.subtract, ALU.mult),
              wait=tsp_w, inc=[(s_dve, 1)])
            xt_consumed[t] = (s_dve, s_dve.n)
            xn_ready = (s_dve, s_dve.n)
            for c in range(8):
                off = (c // 4) * 512 + (c % 4) * 128
                w_d = [xn_ready] if c == 0 else []
                if c == 0:
                    w_d.append(prev_evac)
                E(P, lambda c=c, off=off: nc.tensor.transpose(
                    PS[:, off:off + 128], xn32[:, c * 128:(c + 1) * 128],
                    identf[:, :]), wait=w_d,
                  inc=[(s_pe, 1)] if c in (3, 7) else ())
                if c == 3 or c == 7:
                    tr_done = (s_pe, s_pe.n)
                    base = (c // 4) * 512
                    c0 = c - 3
                    E(V, lambda t=t, base=base, c0=c0: nc.vector.tensor_copy(
                        xn_fm[:, :].rearrange("p (k w) -> p k w", w=2048)
                        [:, c0:c0 + 4, t * 128:(t + 1) * 128],
                        PS[:, base:base + 512].rearrange(
                            "p (k w) -> p k w", w=128)),
                      wait=[tr_done], inc=[(s_dve, 1)])
                    E(V, lambda base=base, c0=c0: nc.vector.tensor_copy(
                        zbuf[:, c0 * 128:(c0 + 4) * 128], PS[:, base:base + 512]),
                      wait=[(s_dve, s_dve.n)], inc=[(s_dve, 1)])
            prev_tr7 = (s_pe, s_pe.n)
            ev_done = (s_dve, s_dve.n)
            for c in range(8):
                w_d = [ev_done, (s_pro, n_pro_dw)] if c == 0 else []
                E(P, lambda c=c: nc.tensor.matmul(
                    PS[:, 3072:3104], zbuf[:, c * 128:(c + 1) * 128],
                    gw_sb[:, c * 32:(c + 1) * 32],
                    start=(c == 0), stop=(c == 7)),
                  wait=w_d, inc=[(s_pe, 1)] if c == 7 else ())
            gate_done = (s_pe, s_pe.n)
            E(V, lambda t=t: nc.vector.tensor_copy(
                logits[:, t * 32:(t + 1) * 32], PS[:, 3072:3104]),
              wait=[gate_done], inc=[(s_dve, 1)])
            prev_evac = (s_dve, s_dve.n)
        phase0_dve = (s_dve, s_dve.n)

        E(A, lambda: nc.scalar.activation(scores[:, :], logits[:, :],
                                          AF.Sigmoid),
          wait=[phase0_dve], inc=[(s_act, 1)])
        sig_done = (s_act, s_act.n)

        sfc = r_sb[:, 0:32]
        gsp = r_sb[:, 32:44]
        mk = r_sb[:, 48:82]
        tw = r_sb[:, 96:128]
        gmask = r_sb[:, 128:136]
        t8 = r_sb[:, 136:144]
        tsA = r_sb[:, 144:145]
        tsB = r_sb[:, 145:146]
        tsC = r_sb[:, 146:147]

        def _rt(fn, extra=()):
            E(V, fn, wait=tuple(extra) + ((s_dve, s_dve.n),), inc=[(s_dve, 1)])

        gm_b = AP(r_sb[:].tensor, r_sb[:].offset + 128,
                  [[r_sb[:].ap[0][0], 128], [1, 8], [0, 4]])
        for t in range(NT):
            sc_t = scores[:, t * 32:(t + 1) * 32]
            _rt(lambda sc_t=sc_t: nc.vector.tensor_add(sfc, sc_t,
                                                       ebias_bc[:, :]),
                extra=[sig_done] if t == 0 else ())
            _rt(lambda: nc.vector.reduce_max(
                gsp[:, 0:8], sfc.rearrange("p (a b) -> p a b", b=4), axis=AX.X))
            _rt(lambda: nc.vector.memset(gsp[:, 8:12], 1e30))
            _rt(lambda: nc.vector.max(t8[:, :], gsp[:, 0:12]))
            _rt(lambda: nc.vector.match_replace(gsp[:, 0:12], t8[:, :],
                                                gsp[:, 0:12], -1e30))
            _rt(lambda: nc.vector.tensor_scalar(
                gmask, gsp[:, 0:8], c_msent, None, ALU.is_equal))
            _rt(lambda: nc.vector.tensor_tensor(
                mk[:, 0:32].rearrange("p (a b) -> p a b", b=4),
                sfc.rearrange("p (a b) -> p a b", b=4), gm_b, ALU.mult))
            _rt(lambda: nc.vector.memset(mk[:, 32:34], 1e30))
            _rt(lambda: nc.vector.max(t8[:, :], mk[:, :]))
            _rt(lambda: nc.vector.match_replace(mk[:, :], t8[:, :],
                                                mk[:, :], -1e30))
            _rt(lambda: nc.vector.tensor_scalar(
                sfc, mk[:, 0:32], c_msent, None, ALU.is_equal))
            _rt(lambda sc_t=sc_t: nc.vector.tensor_mul(tw, sfc, sc_t))
            _rt(lambda: nc.vector.reduce_sum(tsA, tw, axis=AX.X))
            _rt(lambda: nc.vector.tensor_scalar(tsB, tsA, c_tiny, None,
                                                ALU.add))
            _rt(lambda: nc.vector.reciprocal(tsC, tsB))
            _rt(lambda: nc.vector.tensor_scalar(tsB, tsC, c_25, None,
                                                ALU.mult))
            _rt(lambda t=t: nc.vector.tensor_scalar(
                combine[:, t * 32:(t + 1) * 32], tw, tsB, None, ALU.mult))
        routing_done = (s_dve, s_dve.n)

        att_fm = UH
        row_groups = [[0, 2, 4, 6], [1, 3, 5, 7], [8, 10, 12, 14],
                      [9, 11, 13, 15]]
        m_PV_end = {}
        prev_exp1_cur = None
        prev_exp2 = None
        prev_tr_end = None
        trp_evac = {}
        pv_evac = {}
        last_qkv_all = None

        for m in range(NM):
            qkv_evac = {}
            qkv_seq = []
            for which, (wsb, dst, bcol) in enumerate([(wA, qm, 0),
                                                      (wB, kmA, 8)]):
                for c in range(8):
                    off = 3072 + (c % 2) * 512 + ((c // 2) % 2) * 128
                    for kc in range(8):
                        w_d = []
                        if kc == 0:
                            if which == 0 and c == 0:
                                w_d = [phase0_dve, wqkv_pool, routing_done]
                                if m > 0:
                                    w_d.append(m_PV_end[m - 1])
                            elif len(qkv_seq) >= 2:
                                w_d.append(qkv_seq[-2])
                        E(P, lambda wsb=wsb, c=c, kc=kc, m=m, off=off:
                          nc.tensor.matmul(
                              PS[:, off:off + 128],
                              wsb[:, kc * 1024 + c * 128:
                                  kc * 1024 + (c + 1) * 128],
                              xn_fm[:, kc * 2048 + m * 128:
                                    kc * 2048 + (m + 1) * 128],
                              start=(kc == 0), stop=(kc == 7)),
                          wait=w_d, inc=[(s_pe, 1)] if kc == 7 else ())
                    acc_done = (s_pe, s_pe.n)
                    E(V, lambda dst=dst, c=c, off=off, bcol=bcol:
                      nc.vector.tensor_scalar(
                          dst[:, c * 128:(c + 1) * 128], PS[:, off:off + 128],
                          bias_pp[:, bcol + c:bcol + c + 1], None, ALU.add),
                      wait=[acc_done], inc=[(s_dve, 1)])
                    qkv_evac[(which, c)] = (s_dve, s_dve.n)
                    qkv_seq.append((s_dve, s_dve.n))
            for os in range(2):
                off = 3072 + os * 512
                for kc in range(8):
                    w_d = []
                    if kc == 0:
                        w_d = [qkv_evac[(1, 7)], brow_done]
                    E(P, lambda os=os, kc=kc, m=m, off=off: nc.tensor.matmul(
                        PS[:, off:off + 512],
                        xn_fm[:, kc * 2048 + m * 128:kc * 2048 + (m + 1) * 128],
                        wC[:, kc * 1024 + os * 512:kc * 1024 + (os + 1) * 512],
                        start=(kc == 0), stop=False), wait=w_d)
                E(P, lambda os=os, off=off: nc.tensor.matmul(
                    PS[:, off:off + 512], onesb[0:1, :],
                    brow[0:1, os * 512:(os + 1) * 512],
                    start=False, stop=True), inc=[(s_pe, 1)])
                v_done = (s_pe, s_pe.n)
                E(V, lambda os=os, off=off: nc.vector.tensor_copy(
                    vm[:, os * 512:(os + 1) * 512], PS[:, off:off + 512]),
                  wait=[v_done], inc=[(s_dve, 1)])
                qkv_evac[(2, os)] = (s_dve, s_dve.n)
            if m == NM - 1:
                last_qkv_all = (s_dve, s_dve.n)
            kmA_ok = qkv_evac[(1, 7)]
            E(G, lambda: nc.gpsimd.dma_start(out=kmB[0:64, :],
                                             in_=kmA[64:128, :]),
              wait=[kmA_ok], inc=[(s_kmb, 16)])
            E(G, lambda: nc.gpsimd.dma_start(out=kmB[64:128, :],
                                             in_=kmA[0:64, :]),
              inc=[(s_kmb, 16)])
            kmB_ok = (s_kmb, s_kmb.n)
            q_ok = qkv_evac[(2, 1)]
            bt_pe = {}
            bt_act = {}
            for di in range(31):
                slot = di % 2
                hsrc = AP(hvb_d, (m * 31 + di) * 256, [[1, 128], [1, 128]])
                w_d = []
                if di == 0:
                    w_d = [hv_done, brow_done]
                    if m > 0:
                        w_d.append(m_PV_end[m - 1])
                if di >= 2:
                    w_d.append((s_pe, bt_pe[di - 2]))
                shk = s_hk0 if slot == 0 else s_hk1
                E(X, lambda d=hst2[:, slot * 128:(slot + 1) * 128], s=hsrc:
                  nc.sync.dma_start(out=d, in_=s), wait=w_d, inc=[(shk, 16)])
                h_ok = (shk, shk.n)
                pw = [h_ok]
                if di == 0:
                    pw.append(pro_psum_evac if m == 0 else m_PV_end[m - 1])
                    if m == 0:
                        pw.append(phase0_dve)
                if di >= 2:
                    pw.append((s_act, bt_act[di - 2]))
                boff = 2048 + (di % 2) * 512 + ((di // 2) % 2) * 128
                E(P, lambda slot=slot, boff=boff: nc.tensor.matmul(
                    PS[:, boff:boff + 128],
                    revb[:, :], hst2[:, slot * 128:(slot + 1) * 128],
                    start=True, stop=True), wait=pw, inc=[(s_pe, 1)])
                bt_pe[di] = s_pe.n
                E(A, lambda di=di, boff=boff: nc.scalar.copy(
                    UBT[:, (30 - di) * 128:(31 - di) * 128],
                    PS[:, boff:boff + 128]),
                  wait=[(s_pe, bt_pe[di])], inc=[(s_act, 1)])
                bt_act[di] = s_act.n
            bt_ready = (s_act, s_act.n)


            for gi, grp in enumerate(row_groups):
                for ri, fb_a in enumerate(grp):
                    par = fb_a % 2
                    rowid = gi * 4 + ri
                    eslot = rowid % 2
                    Erow = E2[:, eslot * 2048:(eslot + 1) * 2048]
                    for fb_b in range(16):
                        w_d = []
                        if fb_b == 0:
                            w_d = [q_ok, kmB_ok]
                            if prev_exp1_cur is not None:
                                w_d.append(prev_exp1_cur)
                        if fb_b == 8 and prev_exp2 is not None:
                            w_d.append(prev_exp2)
                        km_use = kmA if (fb_b % 2) == par else kmB
                        E(P, lambda fb_a=fb_a, fb_b=fb_b, par=par,
                          km_use=km_use: nc.tensor.matmul(
                              PS[:, fb_b * 128:(fb_b + 1) * 128],
                              qm[par * 64:par * 64 + 64,
                                 (fb_a // 2) * 128:(fb_a // 2 + 1) * 128],
                              km_use[par * 64:par * 64 + 64,
                                     (fb_b // 2) * 128:(fb_b // 2 + 1) * 128],
                              start=True, stop=True,
                              tile_position=(par * 64, 0)),
                          wait=w_d,
                          inc=[(s_pe, 1)] if fb_b in (7, 15) else ())
                        if fb_b == 7:
                            s_half1 = (s_pe, s_pe.n)
                        elif fb_b == 15:
                            s_half2 = (s_pe, s_pe.n)
                    E(A, lambda Erow=Erow: nc.scalar.activation(
                        Erow[:, 0:1024], PS[:, 0:1024], AF.Exp, bias=c_shift,
                        accum_out=partials[:, 0:1]),
                      wait=[s_half1, (s_dve, s_dve.n)], inc=[(s_act, 1)])
                    prev_exp1_cur = (s_act, s_act.n)
                    E(A, lambda Erow=Erow: nc.scalar.activation(
                        Erow[:, 1024:2048], PS[:, 1024:2048], AF.Exp,
                        bias=c_shift, accum_out=partials[:, 1:2]),
                      wait=[s_half2], inc=[(s_act, 1)])
                    prev_exp2 = (s_act, s_act.n)
                    dslot = eslot
                    E(V, lambda: nc.vector.tensor_add(
                        partials[:, 2:3], partials[:, 0:1], partials[:, 1:2]),
                      wait=[prev_exp2, (s_dve, s_dve.n)], inc=[(s_dve, 1)])
                    E(V, lambda: nc.vector.reciprocal(
                        partials[:, 3:4], partials[:, 2:3]),
                      wait=[(s_dve, s_dve.n)], inc=[(s_dve, 1)])
                    E(V, lambda dslot=dslot: nc.vector.tensor_scalar(
                        diag2[:, dslot * 128:(dslot + 1) * 128], identf[:, :],
                        partials[:, 3:4], None, ALU.mult),
                      wait=[(s_dve, s_dve.n)], inc=[(s_dve, 1)])
                    diag_ok = (s_dve, s_dve.n)
                    for qd in range(4):
                        pbase = 2048 + (qd % 2) * 512
                        for j in range(4):
                            fb_b = qd * 4 + j
                            w_d = []
                            if fb_b == 0:
                                w_d.append(diag_ok)
                            if j == 0:
                                if rowid == 0 and qd < 2:
                                    w_d.append(bt_ready)
                                key = (rowid, qd - 2) if qd >= 2 else \
                                      (rowid - 1, qd + 2)
                                if key in trp_evac:
                                    w_d.append(trp_evac[key])
                            E(P, lambda Erow=Erow, fb_b=fb_b, pbase=pbase,
                              j=j, dslot=dslot: nc.tensor.matmul(
                                  PS[:, pbase + j * 128:pbase + (j + 1) * 128],
                                  Erow[:, fb_b * 128:(fb_b + 1) * 128],
                                  diag2[:, dslot * 128:(dslot + 1) * 128],
                                  start=True, stop=True),
                              wait=w_d, inc=[(s_pe, 1)] if j == 3 else ())
                        tq_done = (s_pe, s_pe.n)
                        E(V, lambda fb_a=fb_a, qd=qd, pbase=pbase, ri=ri:
                          nc.vector.tensor_tensor(
                              PT[:, :].rearrange("p (bb w) -> p bb w", w=512)
                              [:, qd * 4:(qd + 1) * 4,
                               ri * 128:(ri + 1) * 128],
                              PS[:, pbase:pbase + 512].rearrange(
                                  "p (bb w) -> p bb w", w=128),
                              UBT[:, (15 - fb_a + qd * 4) * 128:
                                  (15 - fb_a + qd * 4 + 4) * 128].rearrange(
                                  "p (bb w) -> p bb w", w=128),
                              ALU.add), wait=[tq_done, bt_ready],
                          inc=[(s_dve, 1)])
                        trp_evac[(rowid, qd)] = (s_dve, s_dve.n)
                    prev_tr_end = (s_pe, s_pe.n)
                pvpar = grp[0] % 2
                pbase_pv = 3072 if gi in (0, 2) else 3584
                outp = PS[pvpar * 64:pvpar * 64 + 64, pbase_pv:pbase_pv + 512]
                for fb_b in range(16):
                    w_d = []
                    if fb_b == 0:
                        w_d = [trp_evac[(gi * 4 + 3, 3)]]
                        if (m, gi) != (0, 0):
                            pass
                        if gi >= 2:
                            w_d.append(pv_evac[gi - 2])
                    E(P, lambda fb_b=fb_b, outp=outp, pvpar=pvpar:
                      nc.tensor.matmul(
                          outp, vm[:, fb_b * 64:(fb_b + 1) * 64],
                          PT[:, fb_b * 512:(fb_b + 1) * 512],
                          start=(fb_b == 0), stop=(fb_b == 15),
                          tile_position=(0, pvpar * 64)),
                      wait=w_d, inc=[(s_pe, 1)] if fb_b == 15 else ())
                pv_done = (s_pe, s_pe.n)
                c0 = 0 if grp[0] < 8 else 4
                E(V, lambda m=m, c0=c0, pvpar=pvpar, outp=outp:
                  nc.vector.tensor_copy(
                      att_fm[pvpar * 64:pvpar * 64 + 64, :]
                      .rearrange("p (cc w) -> p cc w", w=2048)
                      [:, c0:c0 + 4, m * 128:(m + 1) * 128],
                      outp.rearrange("p (cc w) -> p cc w", w=128)),
                  wait=[pv_done], inc=[(s_dve, 1)])
                pv_evac[gi] = (s_dve, s_dve.n)
            m_PV_end[m] = pv_evac[3]
        attA_done_dve = (s_dve, s_dve.n)
        if DEBUG:
            E(G, lambda: nc.gpsimd.dma_start(
                out=dbg_att.ap().bitcast(BF16)[:, 0:16384][:, 0:16384],
                in_=UH[:, 0:16384]),
              wait=[attA_done_dve], inc=[(s_kmb, 16)])
            E(G, lambda: nc.gpsimd.dma_start(
                out=dbg_xn.ap().bitcast(BF16)[:, 0:16384],
                in_=xn_fm[:, 0:16384]),
              wait=[(s_kmb, s_kmb.n)], inc=[(s_kmb, 16)])
            E(G, lambda: nc.gpsimd.dma_start(
                out=dbg_sc.ap()[:, 0:512], in_=scores[:, :]),
              wait=[routing_done], inc=[(s_kmb, 16)])
            E(G, lambda: nc.gpsimd.dma_start(
                out=dbg_sc.ap()[:, 512:1024], in_=combine[:, :]),
              inc=[(s_kmb, 16)])

        pool_hist.clear()
        _wload("wo", wA, first_wait=last_qkv_all)
        _wload("w1", wB)
        _wload("w2", wC)
        wB_pool = (s_pool, s_pool.n)

        mean0 = r_sb[:, 148:149]
        rstd0 = r_sb[:, 149:150]
        _ln_prev = [prev_ln]

        def _ln0(src):
            E(V, lambda src=src: nc.vector.bn_stats(mvp[:, 0:6], src[:, 0:512]),
              wait=[(s_dve, s_dve.n)], inc=[(s_dve, 1)])
            E(V, lambda src=src: nc.vector.bn_stats(mvp[:, 6:12],
                                                    src[:, 512:1024]),
              wait=[(s_dve, s_dve.n)], inc=[(s_dve, 1)])
            agg_w = [(s_dve, s_dve.n)]
            if _ln_prev[0] is not None:
                agg_w.append(_ln_prev[0])
            E(V, lambda: nc.vector.bn_aggr(mvp[:, 12:14], mvp[:, 0:12]),
              wait=agg_w, inc=[(s_dve, 1)])
            E(V, lambda: nc.vector.tensor_copy(mean0, mvp[:, 12:13]),
              wait=[(s_dve, s_dve.n)], inc=[(s_dve, 1)])
            v_ok = (s_dve, s_dve.n)
            E(A, lambda: nc.scalar.activation(mvp[:, 15:16], mvp[:, 13:14],
                                              AF.Ln),
              wait=[v_ok], inc=[(s_act, 1)])
            _ln_prev[0] = (s_act, s_act.n)
            E(A, lambda: nc.scalar.activation(rstd0, mvp[:, 15:16], AF.Exp,
                                              scale=c_mhalf),
              wait=[(s_act, s_act.n)], inc=[(s_act, 1)])
            return (s_act, s_act.n)

        bx_done = {}
        bz_evac = None
        ff1_last = None
        s_spill = None
        for t in range(NT):
            w_d = [attA_done_dve] if t < 1 else [(s_do, s_do.n)]
            E(G, lambda t=t: nc.gpsimd.dma_start(
                out=x_sb[:, :],
                in_=din["x"].ap()[t * 128:(t + 1) * 128, :]),
              wait=w_d, inc=[(s_x, 16)])
            xB_ready = (s_x, s_x.n)
            xs = x_sb[:, 0:1024]
            for os in range(2):
                for kc in range(8):
                    w_d = []
                    if os == 0 and kc == 0:
                        w_d = [attA_done_dve, wB_pool, brow_done]
                        if bz_evac is not None:
                            w_d.append(bz_evac)
                    E(P, lambda t=t, os=os, kc=kc: nc.tensor.matmul(
                        PS[:, os * 512:(os + 1) * 512],
                        att_fm[:, kc * 2048 + t * 128:
                               kc * 2048 + (t + 1) * 128],
                        wA[:, kc * 1024 + os * 512:kc * 1024 + (os + 1) * 512],
                        start=(kc == 0), stop=False), wait=w_d)
                E(P, lambda os=os: nc.tensor.matmul(
                    PS[:, os * 512:(os + 1) * 512], onesb[0:1, :],
                    brow[0:1, 1024 + os * 512:1024 + (os + 1) * 512],
                    start=False, stop=True),
                  inc=[(s_pe, 1)] if os == 1 else ())
            wo_done = (s_pe, s_pe.n)
            E(V, lambda xs=xs, t=t: nc.vector.tensor_scalar(
                xn32[:, :], xs, stats[:, 2 * t:2 * t + 1],
                stats[:, 2 * t + 1:2 * t + 2], ALU.subtract, ALU.mult),
              wait=[xB_ready, (s_dve, s_dve.n)], inc=[(s_dve, 1)])
            E(V, lambda: nc.vector.tensor_add(zbuf[:, :], PS[:, 0:1024],
                                              xn32[:, :]),
              wait=[wo_done, (s_dve, s_dve.n)], inc=[(s_dve, 1)])
            bz_evac = (s_dve, s_dve.n)
            r_ok = _ln0(zbuf)
            E(V, lambda: nc.vector.tensor_scalar(
                abt[:, :], zbuf[:, :], mean0, rstd0, ALU.subtract, ALU.mult),
              wait=[r_ok, (s_dve, s_dve.n)], inc=[(s_dve, 1)])
            ab_ok = (s_dve, s_dve.n)
            ab_evac = {}
            for c in range(8):
                pb = (2048 if c % 2 == 0 else 3584) + ((c // 2) % 4) * 128
                w_d = [ab_ok] if c == 0 else []
                if c == 0 and t > 0:
                    w_d.append(prev_ff1_in)
                if c >= 2:
                    w_d.append(ab_evac[c - 2])
                E(P, lambda c=c, pb=pb: nc.tensor.transpose(
                    PS[:, pb:pb + 128], abt[:, c * 128:(c + 1) * 128],
                    identf[:, :]), wait=w_d, inc=[(s_pe, 1)])
                tp_ok = (s_pe, s_pe.n)
                E(V, lambda c=c, pb=pb: nc.vector.tensor_copy(
                    ffA[:, c * 128:(c + 1) * 128], PS[:, pb:pb + 128]),
                  wait=[tp_ok], inc=[(s_dve, 1)])
                ab_evac[c] = (s_dve, s_dve.n)
            prev_ff1_in = ab_evac[7]
            ff1_evac = {}
            for c in range(8):
                pb = 1024 + (c % 2) * 512 + ((c // 2) % 4) * 128
                for kc in range(8):
                    w_d = []
                    if kc == 0:
                        w_d = [ab_evac[7]]
                        if c == 0 and ff1_last is not None:
                            w_d.append(ff1_last)
                        if c >= 2:
                            w_d.append(ff1_evac[c - 2])
                    E(P, lambda c=c, kc=kc, pb=pb: nc.tensor.matmul(
                        PS[:, pb:pb + 128],
                        wB[:, kc * 1024 + c * 128:kc * 1024 + (c + 1) * 128],
                        ffA[:, kc * 128:(kc + 1) * 128],
                        start=(kc == 0), stop=(kc == 7)),
                      wait=w_d, inc=[(s_pe, 1)] if kc == 7 else ())
                w1_ok = (s_pe, s_pe.n)
                E(V, lambda c=c, pb=pb: nc.vector.tensor_scalar(
                    ffB[:, c * 128:(c + 1) * 128], PS[:, pb:pb + 128],
                    bias_pp[:, 16 + c:17 + c], None, ALU.add),
                  wait=[w1_ok], inc=[(s_dve, 1)])
                ff1_evac[c] = (s_dve, s_dve.n)
            ff1_last = ff1_evac[7]
            for os in range(2):
                pb = 2560 + os * 512
                for c in range(8):
                    w_d = []
                    if c == 0:
                        w_d = [ff1_evac[7]]
                        if os == 0 and s_spill is not None:
                            w_d.append(s_spill)
                    E(P, lambda os=os, c=c, pb=pb: nc.tensor.matmul(
                        PS[:, pb:pb + 512], ffB[:, c * 128:(c + 1) * 128],
                        wC[:, c * 1024 + os * 512:c * 1024 + (os + 1) * 512],
                        start=(c == 0), stop=False), wait=w_d)
                E(P, lambda os=os, pb=pb: nc.tensor.matmul(
                    PS[:, pb:pb + 512], onesb[0:1, :],
                    brow[0:1, 2048 + os * 512:2048 + (os + 1) * 512],
                    start=False, stop=True),
                  inc=[(s_pe, 1)] if os == 1 else ())
            w2_done = (s_pe, s_pe.n)
            E(V, lambda: nc.vector.tensor_add(zbuf[:, :], PS[:, 2560:3584],
                                              abt[:, :]),
              wait=[w2_done, (s_dve, s_dve.n)], inc=[(s_dve, 1)])
            r_ok = _ln0(zbuf)
            E(V, lambda: nc.vector.tensor_scalar(
                xn32[:, :], zbuf[:, :], mean0, rstd0, ALU.subtract, ALU.mult),
              wait=[r_ok, (s_dve, s_dve.n)], inc=[(s_dve, 1)])
            s_ok = (s_dve, s_dve.n)
            E(G, lambda t=t: nc.gpsimd.dma_start(
                out=s_d.ap()[t * 128:(t + 1) * 128, :], in_=xn32[:, :]),
              wait=[s_ok], inc=[(s_do, 16)])
            s_spill = (s_do, s_do.n)
            bx_done[t] = s_spill
        phaseB_done_pe = (s_pe, s_pe.n)
        phaseB_done_do = (s_do, s_do.n)

        for j in range(4):
            E(V, lambda j=j: nc.vector.memset(acc[:, j * 4096:(j + 1) * 4096],
                                              0.0), inc=[(s_dve, 1)])
        acc_init = (s_dve, s_dve.n)

        pool_hist.clear()
        cast_done = {}
        gu_free = {}

        def _expw(e, par, first_wait=None, dn_wait=None):
            for j in range(24):
                if j < 16:
                    dc, half = j // 2, j % 2
                    src = din["gu"].ap()[e, dc * 128:(dc + 1) * 128,
                                         half * 512:(half + 1) * 512]
                    tgt, toff = guW[par], dc * 1024 + half * 512
                else:
                    jj = j - 16
                    dc, half = jj // 2, jj % 2
                    src = din["dn"].ap()[e, dc * 128:(dc + 1) * 128,
                                         half * 512:(half + 1) * 512]
                    tgt, toff = dnW[par], dc * 1024 + half * 512
                slot = j % 2
                sw = s_w0 if slot == 0 else s_w1
                w_d = []
                if j == 0 and first_wait is not None:
                    w_d.append(first_wait)
                if len(pool_hist) >= 2:
                    w_d.append((s_pool, pool_hist[-2]))
                E(X, lambda d=wstage[:, slot * 512:(slot + 1) * 512], s=src:
                  nc.sync.dma_start(out=d, in_=s), wait=w_d, inc=[(sw, 16)])
                dma_ok = (sw, sw.n)
                cw = [dma_ok]
                if j == 0 and e >= 2:
                    cw.append(gu_free[e - 2])
                if j == 16 and dn_wait is not None:
                    cw.append(dn_wait)
                E(G, lambda tgt=tgt, toff=toff, slot=slot:
                  nc.gpsimd.tensor_copy(
                      tgt[:, toff:toff + 512],
                      wstage[:, slot * 512:(slot + 1) * 512]),
                  wait=cw, inc=[(s_pool, 1)])
                pool_hist.append(s_pool.n)
            cast_done[e] = (s_pool, s_pool.n)

        _expw(0, 0, first_wait=phaseB_done_pe)
        _expw(1, 1)

        silu_hist = {}
        tt1_hist = {}
        stt_hist = {}
        down_end = {}
        for e in range(NE):
            par = e % 2
            h_e = UH[:, par * 8192:(par + 1) * 8192]
            for it in range(16):
                ns, ip = it // 4, it % 4
                q = it % 2
                pg = q * 1024
                pu = q * 1024 + 512
                for dc in range(8):
                    w_d = []
                    if dc == 0:
                        w_d = [cast_done[e]]
                        if it >= 2:
                            w_d += [silu_hist[(e, it - 2)],
                                    tt1_hist[(e, it - 2)]]
                        elif e > 0:
                            w_d += [silu_hist[(e - 1, 14 + q)],
                                    tt1_hist[(e - 1, 14 + q)]]
                        else:
                            w_d.append(phaseB_done_pe)
                    E(P, lambda par=par, dc=dc, ip=ip, ns=ns, pg=pg:
                      nc.tensor.matmul(
                          PS[:, pg:pg + 512],
                          guW[par][:, dc * 1024 + ip * 128:
                                   dc * 1024 + (ip + 1) * 128],
                          xn_fm[:, dc * 2048 + ns * 512:
                                dc * 2048 + (ns + 1) * 512],
                          start=(dc == 0), stop=(dc == 7)),
                      wait=w_d, inc=[(s_pe, 1)] if dc == 7 else ())
                g_ok = (s_pe, s_pe.n)
                for dc in range(8):
                    E(P, lambda par=par, dc=dc, ip=ip, ns=ns, pu=pu:
                      nc.tensor.matmul(
                          PS[:, pu:pu + 512],
                          guW[par][:, dc * 1024 + 512 + ip * 128:
                                   dc * 1024 + 512 + (ip + 1) * 128],
                          xn_fm[:, dc * 2048 + ns * 512:
                                dc * 2048 + (ns + 1) * 512],
                          start=(dc == 0), stop=(dc == 7)),
                      inc=[(s_pe, 1)] if dc == 7 else ())
                u_ok = (s_pe, s_pe.n)
                E(A, lambda q=q, pg=pg: nc.scalar.activation(
                    abt[:, q * 512:(q + 1) * 512], PS[:, pg:pg + 512],
                    AF.Silu), wait=[g_ok], inc=[(s_act, 1)])
                silu_hist[(e, it)] = (s_act, s_act.n)
                w_t = [u_ok, silu_hist[(e, it)]]
                if e == 0 and it == 0:
                    w_t += [acc_init, attA_done_dve]
                E(V, lambda h_e=h_e, q=q, ip=ip, ns=ns, pu=pu:
                  nc.vector.tensor_mul(
                      h_e[:, ip * 2048 + ns * 512:ip * 2048 + (ns + 1) * 512],
                      abt[:, q * 512:(q + 1) * 512], PS[:, pu:pu + 512]),
                  wait=w_t, inc=[(s_dve, 1)])
                tt1_hist[(e, it)] = (s_dve, s_dve.n)
            h_ready = (s_dve, s_dve.n)
            gu_free[e] = (s_pe, s_pe.n)
            for tt in range(NT):
                q2 = tt % 2
                pd = 2048 + q2 * 1024
                for os in range(2):
                    for ip in range(4):
                        w_d = []
                        if os == 0 and ip == 0:
                            w_d = [h_ready]
                            if (e, tt - 2) in stt_hist:
                                w_d.append(stt_hist[(e, tt - 2)])
                            elif e > 0:
                                w_d.append(stt_hist[(e - 1, 14 + q2)])
                        E(P, lambda tt=tt, os=os, ip=ip, pd=pd, par=par:
                          nc.tensor.matmul(
                              PS[:, pd + os * 512:pd + (os + 1) * 512],
                              UH[:, par * 8192 + ip * 2048 + tt * 128:
                                 par * 8192 + ip * 2048 + (tt + 1) * 128],
                              dnW[par][:, ip * 1024 + os * 512:
                                       ip * 1024 + (os + 1) * 512],
                              start=(ip == 0), stop=(ip == 3)),
                          wait=w_d,
                          inc=[(s_pe, 1)] if (os == 1 and ip == 3) else ())
                d_ok = (s_pe, s_pe.n)
                E(V, lambda e=e, tt=tt, pd=pd:
                  nc.vector.scalar_tensor_tensor(
                      acc[:, tt * 1024:(tt + 1) * 1024], PS[:, pd:pd + 1024],
                      combine[:, tt * 32 + e:tt * 32 + e + 1],
                      acc[:, tt * 1024:(tt + 1) * 1024], ALU.mult, ALU.add),
                  wait=[d_ok], inc=[(s_dve, 1)])
                stt_hist[(e, tt)] = (s_dve, s_dve.n)
            down_end[e] = (s_dve, s_dve.n)
            if e + 2 < NE:
                _expw(e + 2, par, dn_wait=(s_pe, s_pe.n))
        pe_all_exp = (s_pe, s_pe.n)

        pool_hist.clear()

        def _shload(key, tgt, first_wait=()):
            for cc in range(16):
                dc, half = cc // 2, cc % 2
                slot = cc % 2
                w_d = []
                if cc == 0:
                    w_d.extend(first_wait)
                if len(pool_hist) >= 2:
                    w_d.append((s_pool, pool_hist[-2]))
                src = din[key].ap()[dc * 128:(dc + 1) * 128,
                                    half * 512:(half + 1) * 512]
                sw = s_w0 if slot == 0 else s_w1
                E(X, lambda d=wstage[:, slot * 512:(slot + 1) * 512], s=src:
                  nc.sync.dma_start(out=d, in_=s), wait=w_d, inc=[(sw, 16)])
                dma_ok = (sw, sw.n)
                E(G, lambda tgt=tgt, dc=dc, half=half, slot=slot:
                  nc.gpsimd.tensor_copy(
                      tgt[:, dc * 1024 + half * 512:
                          dc * 1024 + (half + 1) * 512],
                      wstage[:, slot * 512:(slot + 1) * 512]),
                  wait=[dma_ok], inc=[(s_pool, 1)])
                pool_hist.append(s_pool.n)

        _shload("sg", guW[0], first_wait=[gu_free[NE - 2],
                                          cast_done[NE - 1]])
        _shload("su", guW[1], first_wait=[gu_free[NE - 1]])
        _shload("sd", sdW, first_wait=[pe_all_exp])
        sh_cast = (s_pool, s_pool.n)

        sh_h = UH
        sh_silu = {}
        sh_tt1 = {}
        for it in range(32):
            ns, ip = it // 8, it % 8
            q = it % 2
            pg = q * 1024
            pu = q * 1024 + 512
            for dc in range(8):
                w_d = []
                if dc == 0:
                    w_d = [sh_cast]
                    if it >= 2:
                        w_d += [sh_silu[it - 2], sh_tt1[it - 2]]
                    else:
                        w_d += [down_end[NE - 1],
                                silu_hist[(NE - 1, 14 + q)]]
                E(P, lambda dc=dc, ip=ip, ns=ns, pg=pg: nc.tensor.matmul(
                    PS[:, pg:pg + 512],
                    guW[0][:, dc * 1024 + ip * 128:dc * 1024 + (ip + 1) * 128],
                    xn_fm[:, dc * 2048 + ns * 512:dc * 2048 + (ns + 1) * 512],
                    start=(dc == 0), stop=(dc == 7)),
                  wait=w_d, inc=[(s_pe, 1)] if dc == 7 else ())
            g_ok = (s_pe, s_pe.n)
            for dc in range(8):
                E(P, lambda dc=dc, ip=ip, ns=ns, pu=pu: nc.tensor.matmul(
                    PS[:, pu:pu + 512],
                    guW[1][:, dc * 1024 + ip * 128:dc * 1024 + (ip + 1) * 128],
                    xn_fm[:, dc * 2048 + ns * 512:dc * 2048 + (ns + 1) * 512],
                    start=(dc == 0), stop=(dc == 7)),
                  inc=[(s_pe, 1)] if dc == 7 else ())
            u_ok = (s_pe, s_pe.n)
            E(A, lambda q=q, pg=pg: nc.scalar.activation(
                abt[:, q * 512:(q + 1) * 512], PS[:, pg:pg + 512], AF.Silu),
              wait=[g_ok], inc=[(s_act, 1)])
            sh_silu[it] = (s_act, s_act.n)
            w_t = [u_ok, sh_silu[it]]
            if it < 2:
                w_t.append(down_end[NE - 1])
            E(V, lambda q=q, ip=ip, ns=ns, pu=pu: nc.vector.tensor_mul(
                sh_h[:, ip * 2048 + ns * 512:ip * 2048 + (ns + 1) * 512],
                abt[:, q * 512:(q + 1) * 512], PS[:, pu:pu + 512]),
              wait=w_t, inc=[(s_dve, 1)])
            sh_tt1[it] = (s_dve, s_dve.n)
        sh_h_ready = (s_dve, s_dve.n)

        fin_z = {}
        fin_out = {}
        g_ok2 = None
        for tt in range(NT):
            q2 = tt % 2
            pd = 2048 + q2 * 1024
            for os in range(2):
                for ip in range(8):
                    w_d = []
                    if os == 0 and ip == 0:
                        w_d = [sh_h_ready]
                        if (tt - 2) in fin_z:
                            w_d.append(fin_z[tt - 2])
                        else:
                            w_d.append(stt_hist[(NE - 1, 14 + q2)])
                    E(P, lambda tt=tt, os=os, ip=ip, pd=pd: nc.tensor.matmul(
                        PS[:, pd + os * 512:pd + (os + 1) * 512],
                        sh_h[:, ip * 2048 + tt * 128:ip * 2048 + (tt + 1) * 128],
                        sdW[:, ip * 1024 + os * 512:ip * 1024 + (os + 1) * 512],
                        start=(ip == 0), stop=(ip == 7)),
                      wait=w_d,
                      inc=[(s_pe, 1)] if (os == 1 and ip == 7) else ())
            d_ok = (s_pe, s_pe.n)
            w_x = [phaseB_done_do] if tt < 1 else [fin_out[tt - 1]]
            E(G, lambda tt=tt: nc.gpsimd.dma_start(
                out=x_sb[:, :],
                in_=din["x"].ap()[tt * 128:(tt + 1) * 128, :]),
              wait=w_x, inc=[(s_x, 16)])
            x_ok = (s_x, s_x.n)
            E(G, lambda tt=tt: nc.gpsimd.dma_start(
                out=xn32[:, :], in_=s_d.ap()[tt * 128:(tt + 1) * 128, :]),
              wait=[(s_dve, s_dve.n)], inc=[(s_do, 16)])
            xsr = (s_do, s_do.n)
            w_z = [d_ok, (s_dve, s_dve.n)]
            if g_ok2 is not None:
                w_z.append(g_ok2)
            E(V, lambda tt=tt, pd=pd: nc.vector.tensor_add(
                zbuf[:, :], PS[:, pd:pd + 1024],
                acc[:, tt * 1024:(tt + 1) * 1024]), wait=w_z,
              inc=[(s_dve, 1)])
            fin_z[tt] = (s_dve, s_dve.n)
            E(A, lambda: nc.scalar.activation(abt[:, :], zbuf[:, :], AF.Silu),
              wait=[fin_z[tt]], inc=[(s_act, 1)])
            g_ok2 = (s_act, s_act.n)
            mul_w = [g_ok2, xsr, (s_dve, s_dve.n)]
            if (tt - 1) in fin_out:
                mul_w.append(fin_out[tt - 1])
            E(V, lambda: nc.vector.tensor_mul(
                outt[:, 0:1024], abt[:, :], xn32[:, :]),
              wait=mul_w, inc=[(s_dve, 1)])
            E(V, lambda: nc.vector.tensor_add(
                outt[:, 0:1024], outt[:, 0:1024], x_sb[:, 0:1024]),
              wait=[x_ok, (s_dve, s_dve.n)], inc=[(s_dve, 1)])
            o_ok = (s_dve, s_dve.n)
            E(G, lambda tt=tt: nc.gpsimd.dma_start(
                out=out_d.ap()[tt * 128:(tt + 1) * 128, :],
                in_=outt[:, 0:1024]),
              wait=[o_ok], inc=[(s_do, 16)])
            fin_out[tt] = (s_do, s_do.n)
        if DEBUG:
            E(G, lambda: nc.gpsimd.dma_start(
                out=dbg_acc.ap()[:, :], in_=acc[:, :]),
              wait=[(s_dve, s_dve.n)], inc=[(s_do, 16)])
        final_do = (s_do, s_do.n)
        b.W(X, final_do)
        b.W(G, final_do)

        with nc.Block() as blk:
            def _mk(eng_name):
                def run(eng):
                    for fn, waits, incs in b.q[eng_name]:
                        for h, v in waits:
                            eng.wait_ge(h, v)
                        if fn is None:
                            continue
                        inst = fn()
                        for h, k in incs:
                            inst.then_inc(h, k)
                return run
            blk.sync(_mk("sync"))
            blk.vector(_mk("vector"))
            blk.scalar(_mk("scalar"))
            blk.tensor(_mk("tensor"))
            blk.gpsimd(_mk("gpsimd"))

    return nc


def kernel(**inputs):
    global _PROG
    p = _host_prep(inputs)
    if _PROG is None:
        _PROG = _build()
    nc = _PROG
    shared = {k: v for k, v in p.items() if k != "x_all"}
    in_maps = []
    for c in range(NCORE):
        mm = dict(shared)
        mm["x"] = p["x_all"][c]
        in_maps.append(mm)
    res = run_bass_kernel_spmd(nc, in_maps, core_ids=list(range(NCORE)))
    return np.stack([np.asarray(r["out"]) for r in res.results], axis=0)
